# version 18
# baseline (speedup 1.0000x reference)
import numpy as np
import ml_dtypes

import concourse.bass as bass
import concourse.mybir as mybir
from concourse.bass_utils import run_bass_kernel_spmd

F32 = mybir.dt.float32
BF16 = mybir.dt.bfloat16
I32 = mybir.dt.int32
AF = mybir.ActivationFunctionType
ALU = mybir.AluOpType
AX = mybir.AxisListType

D = 1024
T = 2048
NCORE = 8
HD = 64
TC = 512
NTC = T // TC
NHC = D // 128
EPS = 1e-5
NE = 16
NXP = NE + 2
FF1 = 3072
NIC1 = FF1 // 128
DQ = 1.0 / 127.0

L0_PAIRS = [(0, 4), (1, 5), (2, 6), (3, 7), (8, 12), (9, 13), (10, 14), (11, 15)]


class Sem:
    def __init__(self, nc, name):
        self.h = nc.alloc_semaphore(name=name)
        self.v = 0


class Op:
    __slots__ = ("eng", "fn", "deps", "sem", "val", "name")

    def __init__(self, eng, fn, deps, sem, name):
        self.eng, self.fn, self.deps, self.sem, self.name = eng, fn, deps, sem, name
        self.val = None


class Plan:

    def __init__(self, nc):
        self.nc = nc
        self.ops = []
        self.esem = {e: Sem(nc, f"sem_{e}") for e in ("T", "S", "V")}

    def add(self, eng, fn, deps=(), sem=None, name=""):
        sem = sem if sem is not None else self.esem.get(eng)
        assert sem is not None
        op = Op(eng, fn, [d for d in deps if d is not None], sem, name)
        self.ops.append(op)
        return op

    def emit(self, block):
        for op in self.ops:
            if op.eng in ("G", "Q"):
                n = op.fn(None)
                op.sem.v += 16 * n
            else:
                op.sem.v += 1
            op.val = op.sem.v

        streams = {e: [op for op in self.ops if op.eng == e]
                   for e in ("T", "S", "V", "G", "Q")}

        def make(eng_key):
            ops = streams[eng_key]

            def section(eng):
                waited = {}
                for op in ops:
                    if op.eng in ("V", "S") and op.val > 1:
                        k = id(op.sem)
                        if waited.get(k, -1) < op.val - 1:
                            eng.wait_ge(op.sem.h, op.val - 1)
                            waited[k] = op.val - 1
                    want = {}
                    for d in op.deps:
                        k = id(d.sem)
                        if k not in want or want[k][1] < d.val:
                            want[k] = (d.sem, d.val)
                    for k, (sm, v) in want.items():
                        if waited.get(k, -1) < v:
                            eng.wait_ge(sm.h, v)
                            waited[k] = v
                    last = op.fn(eng)
                    if op.eng in ("G", "Q"):
                        for ins in (last if isinstance(last, (list, tuple)) else
                                    ([last] if last is not None else [])):
                            ins.then_inc(op.sem.h, 16)
                    else:
                        if last is None:
                            last = eng.drain()
                        last.then_inc(op.sem.h, 1)
            return section

        block.gpsimd(make("G"))
        block.sync(make("Q"))
        block.tensor(make("T"))
        block.scalar(make("S"))
        block.vector(make("V"))


class Region:
    def __init__(self, tensor, ncols, name):
        self.t, self.ncols, self.name = tensor, ncols, name
        self.off = 0
        self.marks = []
        self.peak = 0

    def alloc(self, n):
        assert self.off + n <= self.ncols, \
            f"{self.name} overflow: {self.off}+{n} > {self.ncols}"
        o = self.off
        self.off += n
        self.peak = max(self.peak, self.off)
        return o

    def push(self):
        self.marks.append(self.off)

    def pop(self):
        self.off = self.marks.pop()


def build_program(debug=False):
    nc = bass.Bass("TRN2", target_bir_lowering=False)
    dt = nc.dram_tensor

    x_in = dt("x_in", [D, T], F32, kind="ExternalInput").ap()
    out_t = dt("out", [D, T], F32, kind="ExternalOutput").ap()
    x_res = dt("x_res", [D, T], F32, kind="Internal").ap()
    s_g = dt("s_g", [NXP, 2, 128, 2048], BF16, kind="Internal").ap()
    s_u = dt("s_u", [NXP, 2, 128, 2048], BF16, kind="Internal").ap()
    s_d = dt("s_d", [NXP, 128, 4096], BF16, kind="Internal").ap()

    l0_wq2 = dt("l0_wq2", [8, 128, 1024], F32, kind="ExternalInput").ap()
    l0_wk2 = dt("l0_wk2", [2, 128, 1024], F32, kind="ExternalInput").ap()
    l0_wv2 = dt("l0_wv2", [128, 2048], F32, kind="ExternalInput").ap()
    l0_wo2 = dt("l0_wo2", [8, 128, 1024], F32, kind="ExternalInput").ap()
    l0_eb = dt("l0_eb", [128, 16], F32, kind="ExternalInput").ap()
    l0_g2 = dt("l0_g2", [NE, 2, 128, 2048], F32, kind="ExternalInput").ap()
    l0_u2 = dt("l0_u2", [NE, 2, 128, 2048], F32, kind="ExternalInput").ap()
    l0_d2 = dt("l0_d2", [NE, 128, 4096], F32, kind="ExternalInput").ap()
    l0_sg2 = dt("l0_sg2", [2, 2, 128, 2048], F32, kind="ExternalInput").ap()
    l0_su2 = dt("l0_su2", [2, 2, 128, 2048], F32, kind="ExternalInput").ap()
    l0_sd2 = dt("l0_sd2", [2, 128, 4096], F32, kind="ExternalInput").ap()

    l1_wq2 = dt("l1_wq2", [8, 128, 1024], F32, kind="ExternalInput").ap()
    l1_wk2 = dt("l1_wk2", [8, 128, 1024], F32, kind="ExternalInput").ap()
    l1_wv2 = dt("l1_wv2", [128, 8192], F32, kind="ExternalInput").ap()
    l1_bqk = dt("l1_bqk", [128, 16], F32, kind="ExternalInput").ap()
    l1_bvbc = dt("l1_bvbc", [128, D], F32, kind="ExternalInput").ap()
    l1_wo2 = dt("l1_wo2", [8, 128, 1024], F32, kind="ExternalInput").ap()
    l1_bout = dt("l1_bout", [128, NHC], F32, kind="ExternalInput").ap()
    l1_g2 = dt("l1_g2", [NIC1, 128, 1024], I32, kind="ExternalInput").ap()
    l1_u2 = dt("l1_u2", [NIC1, 128, 1024], I32, kind="ExternalInput").ap()
    l1_d2 = dt("l1_d2", [NHC, 128, 3072], I32, kind="ExternalInput").ap()

    ident = dt("ident", [128, 128], F32, kind="ExternalInput").ap()
    colones = dt("colones", [128, 128], F32, kind="ExternalInput").ap()
    rwp_b = dt("rwp_b", [128, 1024], BF16, kind="ExternalInput").ap()
    r2t = dt("r2t", [128, 128], BF16, kind="ExternalInput").ap()
    cs2 = dt("cs2", [128, 2 * T], F32, kind="ExternalInput").ap()
    cmask = dt("cmask", [128, 4 * TC], BF16, kind="ExternalInput").ap()
    oneh = dt("oneh", [16, 16 * 128], BF16, kind="ExternalInput").ap()
    epsc = dt("epsc", [128, 1], F32, kind="ExternalInput").ap()
    br0 = dt("br0", [128, 128], BF16, kind="ExternalInput").ap()

    dbg = {}

    def dbg_out(name, shape):
        if debug:
            dbg[name] = dt(name, shape, F32, kind="ExternalOutput").ap()
            return dbg[name]
        return None

    d_xn = [dbg_out("d_xn0", [D, T]), dbg_out("d_xn1", [D, T])]
    d_sc = dbg_out("d_sc0", [128, 64 * NTC])
    d_C = dbg_out("d_C0", [16, T])
    d_g = [dbg_out("d_g0", [D, T]), dbg_out("d_g1", [D, T])]
    d_oall = [dbg_out("d_oall0", [128, NHC * TC]), dbg_out("d_oall1", [128, NHC * TC])]
    d_sums = [dbg_out("d_sums0", [32, TC]), dbg_out("d_sums1", [32, TC])]

    A16 = 55296
    A32 = 5120
    with (
        nc.sbuf_tensor("xn_sb", [128, NHC * T], BF16) as xn_sb,
        nc.sbuf_tensor("g_sb", [128, NHC * T], BF16) as g_sb,
        nc.sbuf_tensor("a16t", [128, A16], BF16) as a16t,
        nc.sbuf_tensor("a32t", [128, A32], F32) as a32t,
        nc.sbuf_tensor("cst32", [128, 192], F32) as cst32,
        nc.sbuf_tensor("cst16", [128, 1280], BF16) as cst16,
        nc.sbuf_tensor("axr", [128, 2048], mybir.dt.float32r) as axr,
        nc.sbuf_tensor("cstr", [128, 128], mybir.dt.float32r) as cstr,
        nc.sbuf_tensor("oneh_sb", [16, 16 * 128], BF16) as oneh_sb,
        nc.psum_tensor("pst0", [128, 1024], F32) as pst0,
        nc.psum_tensor("pst1", [128, 1024], F32) as pst1,
        nc.psum_tensor("pb4", [128, 512], F32) as pb4,
        nc.psum_tensor("pb5", [128, 512], F32) as pb5,
        nc.psum_tensor("pb6", [128, 512], F32) as pb6,
        nc.psum_tensor("pb7", [128, 512], F32) as pb7,
        nc.Block() as block,
    ):
        pst = [pst0, pst1]
        pb0, pb1 = pst0[:, 0:512], pst0[:, 512:1024]
        pb2, pb3 = pst1[:, 0:512], pst1[:, 512:1024]
        pb = [pb0, pb1, pb2, pb3, pb4, pb5, pb6, pb7]
        P = Plan(nc)
        a16 = Region(a16t, A16, "a16")
        a32 = Region(a32t, A32, "a32")

        sW = [Sem(nc, f"sW{i}") for i in range(4)]
        sWu = [Sem(nc, f"sWu{i}") for i in range(2)]
        sWd = [Sem(nc, f"sWd{i}") for i in range(2)]
        sWB = [Sem(nc, f"sWB{i}") for i in range(4)]
        sXa = [Sem(nc, f"sXa{i}") for i in range(3)]
        sXb = [Sem(nc, f"sXb{i}") for i in range(2)]
        sXl = [Sem(nc, f"sXl{i}") for i in range(2)]
        sO = Sem(nc, "sO")
        sO2 = Sem(nc, "sO2")
        sCst = Sem(nc, "sCst")
        sDbg = Sem(nc, "sDbg")
        dbg_prev = [None]
        sDbq = Sem(nc, "sDbq")
        dbq_prev = [None]
        sCS = Sem(nc, "sCS")
        sKW = [Sem(nc, "sKW0"), Sem(nc, "sKW1")]
        sVW = Sem(nc, "sVW")
        sWO = [Sem(nc, "sWO0"), Sem(nc, "sWO1")]
        sQW = [Sem(nc, "sQW0"), Sem(nc, "sQW1")]

        C_ID, C_EB, C_BQK, C_BOUT, C_EPS = 0, 128, 144, 160, 168
        B_R2T, B_BR0, B_RWP = 0, 128, 256

        def ld_consts(g):
            if g is None:
                return 10
            return [
                g.dma_start(out=cst32[:, C_ID:C_ID + 128], in_=ident[:, :]),
                g.dma_start(out=cst32[:, C_EB:C_EB + 16], in_=l0_eb[:, :]),
                g.dma_start(out=cst32[:, C_BQK:C_BQK + 16], in_=l1_bqk[:, :]),
                g.dma_start(out=cst32[:, C_BOUT:C_BOUT + 8], in_=l1_bout[:, :]),
                g.dma_start(out=cst16[:, B_R2T:B_R2T + 128], in_=r2t[:, :]),
                g.dma_start(out=oneh_sb[:, :], in_=oneh[:, :]),
                g.dma_start(out=cst32[:, C_EPS:C_EPS + 1], in_=epsc[:, :]),
                g.dma_start(out=cst16[:, B_BR0:B_BR0 + 128], in_=br0[:, :]),
                g.dma_start(out=cst16[:, B_RWP:B_RWP + 1024], in_=rwp_b[:, :]),
                g.dma_start(out=cstr[:, :], in_=colones[:, :]),
            ]
        op_cst = P.add("G", ld_consts, sem=sCst, name="consts")

        fence0 = [op_cst]

        def phase_A(L, fence):
            xsrc = x_in if L == 0 else x_res
            o_bcs = A16 - 1024
            a32.push()
            o_x2 = [a32.alloc(TC) for _ in range(2)]
            o_mean = a32.alloc(TC)
            o_var = a32.alloc(TC)
            o_rstd = a32.alloc(TC)
            o_nmr = a32.alloc(TC)
            o_t1 = a32.alloc(TC)
            o_xt2 = a32.alloc(TC)
            if L == 0:
                o_srT = a32.alloc(512)
                o_sraw = a32.alloc(64)
                o_z = a32.alloc(64)
                o_sc = a32.alloc(64)
                o_scb = a32.alloc(64)
                o_rs = a32.alloc(160)
                o_Ct = a32.alloc(64)
            AXA = [0, TC, 2 * TC]
            AXQ = 3 * TC

            op_msA = P.add("V", lambda v: v.memset(
                a16t[:, o_bcs:o_bcs + 1024], 0.0), deps=fence, name=f"A{L}ms")

            prev = {}
            ct_ops = []
            for tc in range(NTC):
                for hc in range(NHC):
                    gi = tc * NHC + hc
                    xp = gi % 3

                    def f_ld(g, tc=tc, hc=hc, xp=xp):
                        if g is None:
                            return 1
                        return g.dma_start(
                            out=axr[:, AXA[xp]:AXA[xp] + TC],
                            in_=xsrc[hc * 128:(hc + 1) * 128, tc * TC:(tc + 1) * TC])
                    dl = [prev.get(("p1g", gi - 3))]
                    if gi < 3:
                        dl = dl + fence
                    op_ld = P.add("G", f_ld, deps=dl, sem=sXa[xp],
                                  name=f"A{L}ld{tc}.{hc}")

                    def f_sq(s, xp=xp):
                        return s.activation(
                            axr[:, AXQ:AXQ + TC],
                            axr[:, AXA[xp]:AXA[xp] + TC].bitcast(F32), AF.Square)
                    op_sq = P.add("S", f_sq,
                                  deps=[op_ld, prev.get(("mm", tc, hc - 1))],
                                  name=f"A{L}sq{tc}.{hc}")

                    def f_mm1(t, hc=hc, xp=xp):
                        t.matmul(pb4[:, :], cstr[:, 0:128],
                                 axr[:, AXA[xp]:AXA[xp] + TC],
                                 start=(hc == 0), stop=(hc == NHC - 1))
                        return t.matmul(pb5[:, :], cstr[:, 0:128],
                                        axr[:, AXQ:AXQ + TC],
                                        start=(hc == 0), stop=(hc == NHC - 1))
                    dm = [op_ld, op_sq, prev.get(("srow", tc - 1))]
                    op_mm = P.add("T", f_mm1, deps=dm, name=f"A{L}mm{tc}.{hc}")
                    prev[("p1g", gi)] = op_mm
                    prev[("mm", tc, hc)] = op_mm

                def f_srow(v):
                    v.tensor_scalar(a32t[0:1, o_mean:o_mean + TC], pb4[0:1, :],
                                    1.0 / D, None, ALU.mult)
                    v.drain()
                    v.tensor_scalar(a32t[0:1, o_var:o_var + TC], pb5[0:1, :],
                                    1.0 / D, None, ALU.mult)
                    v.drain()
                    v.tensor_tensor(a32t[0:1, o_t1:o_t1 + TC],
                                    a32t[0:1, o_mean:o_mean + TC],
                                    a32t[0:1, o_mean:o_mean + TC], ALU.mult)
                    v.drain()
                    return v.tensor_tensor(a32t[0:1, o_var:o_var + TC],
                                           a32t[0:1, o_var:o_var + TC],
                                           a32t[0:1, o_t1:o_t1 + TC], ALU.subtract)
                op_srow = P.add("V", f_srow, deps=[prev[("mm", tc, NHC - 1)]],
                                name=f"A{L}srow{tc}")
                prev[("srow", tc)] = op_srow

                op_sqrt = P.add("S", lambda s: s.activation(
                    a32t[0:1, o_t1:o_t1 + TC], a32t[0:1, o_var:o_var + TC],
                    AF.Sqrt, bias=cst32[0:1, C_EPS:C_EPS + 1]),
                    deps=[op_srow], name=f"A{L}sqrt{tc}")

                def f_rstd(v):
                    v.reciprocal(a32t[0:1, o_rstd:o_rstd + TC],
                                 a32t[0:1, o_t1:o_t1 + TC])
                    v.drain()
                    return v.scalar_tensor_tensor(
                        a32t[0:1, o_nmr:o_nmr + TC],
                        a32t[0:1, o_mean:o_mean + TC], -1.0,
                        a32t[0:1, o_rstd:o_rstd + TC], ALU.mult, ALU.mult)
                op_rstd = P.add("V", f_rstd, deps=[op_sqrt], name=f"A{L}rstd{tc}")

                def f_brow(v):
                    with nc.allow_low_precision(reason="LN scale rows -> bf16 bcast"):
                        v.tensor_copy(a16t[0:1, o_bcs:o_bcs + TC],
                                      a32t[0:1, o_rstd:o_rstd + TC])
                        v.drain()
                        return v.tensor_copy(a16t[0:1, o_bcs + TC:o_bcs + 2 * TC],
                                             a32t[0:1, o_nmr:o_nmr + TC])
                op_brow = P.add("V", f_brow,
                                deps=[op_rstd, op_msA, prev.get(("bcmm", tc - 1))],
                                name=f"A{L}brow{tc}")

                def f_bc(t):
                    t.matmul(pb6[:, :], cst16[:, B_BR0:B_BR0 + 128],
                             a16t[:, o_bcs:o_bcs + TC], start=True, stop=True)
                    return t.matmul(pb7[:, :], cst16[:, B_BR0:B_BR0 + 128],
                                    a16t[:, o_bcs + TC:o_bcs + 2 * TC],
                                    start=True, stop=True)
                op_bc = P.add("T", f_bc, deps=[op_brow, prev.get(("xnl", tc - 1))],
                              name=f"A{L}bc{tc}")
                prev[("bcmm", tc)] = op_bc

                xn_ops = []
                for hc in range(NHC):
                    xp = hc % 2

                    def f_ld2(q, tc=tc, hc=hc, xp=xp):
                        if q is None:
                            return 1
                        return q.dma_start(
                            out=a32t[:, o_x2[xp]:o_x2[xp] + TC],
                            in_=xsrc[hc * 128:(hc + 1) * 128, tc * TC:(tc + 1) * TC])
                    dl2 = [prev.get(("xn", tc, hc - 2)),
                           prev.get(("xn", tc - 1, NHC - 2 + hc)) if hc < 2 else None]
                    if tc == 0 and hc < 2:
                        dl2 = dl2 + fence
                    op_ld2 = P.add("Q", f_ld2, deps=dl2, sem=sXb[xp],
                                   name=f"A{L}ld2{tc}.{hc}")

                    def f_xn(v, hc=hc, tc=tc, xp=xp):
                        v.tensor_tensor(a32t[:, o_xt2:o_xt2 + TC],
                                        a32t[:, o_x2[xp]:o_x2[xp] + TC],
                                        pb6[:, :], ALU.mult)
                        v.drain()
                        return v.tensor_tensor(
                            xn_sb[:, hc * T + tc * TC: hc * T + tc * TC + TC],
                            a32t[:, o_xt2:o_xt2 + TC], pb7[:, :], ALU.add)
                    op_xn = P.add("V", f_xn, deps=[op_bc, op_ld2],
                                  name=f"A{L}xn{tc}.{hc}")
                    prev[("xn", tc, hc)] = op_xn
                    xn_ops.append(op_xn)
                prev[("xnl", tc)] = xn_ops[-1]

                if L == 0:
                    def f_rsc(t, tc=tc):
                        for hc in range(NHC):
                            li = t.matmul(
                                pst0[:, 0:512],
                                cst16[:, B_RWP + hc * 128: B_RWP + (hc + 1) * 128],
                                xn_sb[:, hc * T + tc * TC: hc * T + tc * TC + TC],
                                start=(hc == 0), stop=(hc == NHC - 1))
                        return li
                    op_rsc = P.add("T", f_rsc,
                                   deps=[prev[("xnl", tc)],
                                         prev.get(("rscE", tc - 1))],
                                   name=f"A0rsc{tc}")

                    def f_rawT(v):
                        return v.tensor_copy(a32t[0:16, o_srT:o_srT + TC],
                                             pst0[0:16, 0:512])
                    op_rawT = P.add("V", f_rawT, deps=[op_rsc], name=f"A0rawT{tc}")
                    prev[("rscE", tc)] = op_rawT

                    def f_rtr(t):
                        for tt in range(4):
                            li = t.transpose(
                                pst0[:, 512 + tt * 16:512 + (tt + 1) * 16],
                                a32t[0:16, o_srT + tt * 128:
                                     o_srT + (tt + 1) * 128],
                                cst32[0:16, C_ID:C_ID + 16])
                        return li
                    op_rtr = P.add("T", f_rtr,
                                   deps=[op_rawT, prev.get(("rtrE", tc - 1))],
                                   name=f"A0rtr{tc}")

                    def f_rawe(v):
                        return v.tensor_copy(a32t[:, o_sraw:o_sraw + 64],
                                             pst0[:, 512:576])
                    op_rawe = P.add("V", f_rawe, deps=[op_rtr], name=f"A0rawe{tc}")
                    prev[("rtrE", tc)] = op_rawe

                    sig_ops = []
                    for tt in range(4):
                        def f_sig(s, tt=tt):
                            return s.activation(
                                a32t[:, o_sc + tt * 16: o_sc + (tt + 1) * 16],
                                a32t[:, o_sraw + tt * 16: o_sraw + (tt + 1) * 16],
                                AF.Sigmoid)
                        sig_ops.append(P.add("S", f_sig, deps=[op_rawe],
                                             name=f"A0sig{tc}.{tt}"))

                    def f_scb(v):
                        for tt in range(4):
                            li = v.tensor_tensor(
                                a32t[:, o_scb + tt * 16: o_scb + (tt + 1) * 16],
                                a32t[:, o_sc + tt * 16: o_sc + (tt + 1) * 16],
                                cst32[:, C_EB:C_EB + 16], ALU.add)
                        return li
                    op_scb = P.add("V", f_scb, deps=sig_ops, name=f"A0scb{tc}")

                    rA, rB, rGS, rMID, rLH, rM2G, rGM, rCUR, rCI, rSB = \
                        0, 32, 64, 80, 96, 112, 116, 0, 32, 34
                    S_ = a32t[:, o_scb:o_scb + 64]

                    def f_rt1(v):
                        v.tensor_tensor(a32t[:, o_rs + rA:o_rs + rA + 32],
                                        S_[:, 0:64:2], S_[:, 1:64:2], ALU.max)
                        v.drain()
                        v.tensor_tensor(a32t[:, o_rs + rB:o_rs + rB + 32],
                                        S_[:, 0:64:2], S_[:, 1:64:2], ALU.min)
                        v.drain()
                        A2 = a32t[:, o_rs + rA:o_rs + rA + 32]
                        B2 = a32t[:, o_rs + rB:o_rs + rB + 32]
                        v.tensor_tensor(a32t[:, o_rs + rGS:o_rs + rGS + 16],
                                        A2[:, 0:32:2], A2[:, 1:32:2], ALU.max)
                        v.drain()
                        v.tensor_tensor(a32t[:, o_rs + rMID:o_rs + rMID + 16],
                                        A2[:, 0:32:2], A2[:, 1:32:2], ALU.min)
                        v.drain()
                        v.tensor_tensor(a32t[:, o_rs + rLH:o_rs + rLH + 16],
                                        B2[:, 0:32:2], B2[:, 1:32:2], ALU.max)
                        v.drain()
                        v.tensor_tensor(a32t[:, o_rs + rMID:o_rs + rMID + 16],
                                        a32t[:, o_rs + rMID:o_rs + rMID + 16],
                                        a32t[:, o_rs + rLH:o_rs + rLH + 16], ALU.max)
                        v.drain()
                        return v.tensor_tensor(a32t[:, o_rs + rGS:o_rs + rGS + 16],
                                               a32t[:, o_rs + rGS:o_rs + rGS + 16],
                                               a32t[:, o_rs + rMID:o_rs + rMID + 16],
                                               ALU.add)
                    op_rt1 = P.add("V", f_rt1, deps=[op_scb], name=f"A0rt1{tc}")

                    def f_rt2(v):
                        GS = a32t[:, o_rs + rGS:o_rs + rGS + 16]
                        v.tensor_tensor(a32t[:, o_rs + rMID:o_rs + rMID + 8],
                                        GS[:, 0:16:2], GS[:, 1:16:2], ALU.max)
                        v.drain()
                        v.tensor_tensor(a32t[:, o_rs + rLH:o_rs + rLH + 8],
                                        GS[:, 0:16:2], GS[:, 1:16:2], ALU.min)
                        v.drain()
                        PH = a32t[:, o_rs + rMID:o_rs + rMID + 8]
                        PL = a32t[:, o_rs + rLH:o_rs + rLH + 8]
                        v.tensor_tensor(a32t[:, o_rs + rM2G:o_rs + rM2G + 4],
                                        PH[:, 0:8:2], PH[:, 1:8:2], ALU.min)
                        v.drain()
                        v.tensor_tensor(a32t[:, o_rs + rLH + 8:o_rs + rLH + 12],
                                        PL[:, 0:8:2], PL[:, 1:8:2], ALU.max)
                        v.drain()
                        v.tensor_tensor(a32t[:, o_rs + rM2G:o_rs + rM2G + 4],
                                        a32t[:, o_rs + rM2G:o_rs + rM2G + 4],
                                        a32t[:, o_rs + rLH + 8:o_rs + rLH + 12],
                                        ALU.max)
                        v.drain()
                        for gg in range(4):
                            v.tensor_tensor(
                                a32t[:, o_rs + rGM + gg:o_rs + rGM + gg + 13:4],
                                GS[:, gg:16:4],
                                a32t[:, o_rs + rM2G:o_rs + rM2G + 4], ALU.is_ge)
                        v.drain()
                        for j in range(4):
                            li = v.tensor_tensor(
                                a32t[:, o_z + j:o_z + j + 61:4],
                                S_[:, j:64:4],
                                a32t[:, o_rs + rGM:o_rs + rGM + 16], ALU.mult)
                        return li
                    op_rt2 = P.add("V", f_rt2, deps=[op_rt1], name=f"A0rt2{tc}")

                    def f_top4(v):
                        cur = a32t[:, o_rs + 0:o_rs + 64]
                        cur3 = cur.rearrange("p (g e) -> p g e", e=16)
                        selb = a32t[:, o_rs + 64:o_rs + 128]
                        selb3 = selb.rearrange("p (g e) -> p g e", e=16)
                        ci = a32t[:, o_rs + 128:o_rs + 132]
                        ci3 = ci.rearrange("p (g e) -> p g e", e=1)
                        cib = ci3.broadcast_to((128, 4, 16))
                        ci2 = a32t[:, o_rs + 132:o_rs + 136]
                        ci2b = ci2.rearrange("p (g e) -> p g e", e=1)\
                            .broadcast_to((128, 4, 16))
                        mk = a32t[:, o_z:o_z + 64]
                        mk3 = mk.rearrange("p (g e) -> p g e", e=16)
                        sc3 = a32t[:, o_sc:o_sc + 64]
                        Ct3 = a32t[:, o_Ct:o_Ct + 64].rearrange(
                            "p (g e) -> p g e", e=16)
                        v.tensor_copy(cur, mk)
                        for _ in range(3):
                            v.drain()
                            v.reduce_max(ci3, cur3, axis=AX.X)
                            v.drain()
                            v.tensor_tensor(selb3, cur3, cib, ALU.is_ge)
                            v.drain()
                            v.scalar_tensor_tensor(cur, selb, -1e9, cur,
                                                   ALU.mult, ALU.add)
                        v.drain()
                        v.reduce_max(ci3, cur3, axis=AX.X)
                        v.drain()
                        v.tensor_tensor(selb3, mk3, cib, ALU.is_ge)
                        v.drain()
                        v.tensor_tensor(cur, selb, sc3, ALU.mult)
                        v.drain()
                        v.reduce_sum(ci3, cur3, axis=AX.X)
                        v.drain()
                        v.tensor_scalar(ci, ci, 1e-20, None, ALU.add)
                        v.drain()
                        v.reciprocal(ci2, ci)
                        v.drain()
                        return v.scalar_tensor_tensor(Ct3, cur3, 2.5, ci2b,
                                                      ALU.mult, ALU.mult)
                    op_t4 = P.add("V", f_top4, deps=[op_rt2], name=f"A0t4{tc}")

                    for tt in range(4):

                        def f_ctr(t, tt=tt):
                            dstb = pst1[:, 0:128] if tt % 2 == 0 else pst1[:, 512:640]
                            return t.transpose(
                                dstb[0:16, 0:128],
                                a32t[:, o_Ct + tt * 16: o_Ct + (tt + 1) * 16],
                                cst32[:, C_ID:C_ID + 128])
                        dtr = [op_t4]
                        if tt >= 2:
                            dtr.append(ct_ops[-2][0])
                        else:
                            dtr.append(prev.get(("ctE", tc - 1)))
                        op_ctr = P.add("T", f_ctr, deps=dtr, name=f"A0ctr{tc}.{tt}")

                        def f_cte(v, tt=tt, tc=tc):
                            srcb = pst1[:, 0:128] if tt % 2 == 0 else pst1[:, 512:640]
                            return v.tensor_copy(
                                cfm_sb[0:16, tc * TC + tt * 128:
                                       tc * TC + (tt + 1) * 128],
                                srcb[0:16, 0:128])
                        op_cte = P.add("V", f_cte, deps=[op_ctr],
                                       name=f"A0cte{tc}.{tt}")
                        ct_ops.append((op_cte, op_ctr))
                    prev[("ctE", tc)] = ct_ops[-1][0]

            res = dict(xn_last=prev[("xnl", NTC - 1)])
            res["xn_tc"] = [prev[("xnl", t_)] for t_ in range(NTC)]
            res["lasts"] = [res["xn_last"]]
            if L == 0:
                res["c_last"] = ct_ops[-1][0]
                res["c_tc"] = [ct_ops[4 * t_ + 3][0] for t_ in range(NTC)]
                res["lasts"].append(res["c_last"])
            else:
                res["c_last"] = None
                res["lasts"].append(prev[("srow", NTC - 1)])
            a32.pop()
            return res

        def phase_B0(A, ldfence):
            wb_ring = [None, None, None, None]
            wb_i = [0]
            wb_map = {}
            a16.push()
            o_gb = [a16.alloc(2048) for _ in range(2)]
            o_ub = [a16.alloc(2048) for _ in range(2)]
            o_db = [a16.alloc(4096) for _ in range(2)]
            o_hp = a16.alloc(NXP * 2048)
            a32.push()
            o_silu = [a32.alloc(TC) for _ in range(2)]
            o_stg = [a32.alloc(2048) for _ in range(2)]

            prev = {}
            hidx = 0
            gevac_prev = None
            dn_last_prev = None
            for tc in range(NTC):
                hlast = []
                cbc0 = None

                def f_cbc0(t, tc=tc):
                    return t.matmul(pb4[:, :], oneh_sb[0:16, 0:128],
                                    cfm_sb[0:16, tc * TC:(tc + 1) * TC],
                                    start=True, stop=True)
                dc0 = [A["c_last"], prev.get(("pcb", 0)), gevac_prev]
                if tc == 0:
                    dc0 += A["lasts"]
                cbc0 = P.add("T", f_cbc0, deps=dc0, name=f"B0cbc{tc}.0")
                prev[("cbc", 0)] = cbc0

                for e in range(NXP):
                    hp_e = o_hp + e * 2048
                    e_h1 = []
                    for half in range(2):
                        wp = hidx % 2
                        if tc == 0:
                            if e < NE:
                                gsrc = l0_g2[e, half]
                                usrc = l0_u2[e, half]
                            else:
                                gsrc = l0_sg2[e - NE, half]
                                usrc = l0_su2[e - NE, half]
                        else:
                            gsrc = s_g[e, half]
                            usrc = s_u[e, half]

                        def f_ldg(q, gsrc=gsrc, wp=wp):
                            if q is None:
                                return 1
                            return q.dma_start(
                                out=a16t[:, o_gb[wp]:o_gb[wp] + 2048], in_=gsrc)

                        def f_ldu(q, usrc=usrc, wp=wp):
                            if q is None:
                                return 1
                            return q.dma_start(
                                out=a16t[:, o_ub[wp]:o_ub[wp] + 2048], in_=usrc)
                        bp = prev.get(("gubuf", hidx - 2))
                        fen = ldfence if hidx < 2 else []
                        wbg_b = prev.get(("wbgb", wp))
                        wbu_b = prev.get(("wbub", wp))
                        if tc == 0:
                            opldg = P.add("G", f_ldg, deps=[bp, wbg_b] + fen,
                                          sem=sW[wp], name=f"B0ldg{tc}.{e}.{half}")

                            def f_ldur(q, usrc=usrc, wp=wp):
                                if q is None:
                                    return 1
                                return q.dma_start(
                                    out=a32t[:, o_stg[wp]:o_stg[wp] + 2048],
                                    in_=usrc)
                            opldur = P.add("Q", f_ldur,
                                           deps=[prev.get(("stg", wp))] + fen,
                                           sem=sWu[wp],
                                           name=f"B0ldur{tc}.{e}.{half}")

                            def f_ucast(v, wp=wp):
                                with nc.allow_low_precision(
                                        reason="weight stream bf16 cast"):
                                    return v.tensor_copy(
                                        a16t[:, o_ub[wp]:o_ub[wp] + 2048],
                                        a32t[:, o_stg[wp]:o_stg[wp] + 2048])
                            opldu = P.add("V", f_ucast,
                                          deps=[opldur, bp, wbu_b],
                                          name=f"B0ucast{e}.{half}")
                            prev[("stg", wp)] = opldu

                            def f_wbg(q, e=e, half=half, wp=wp):
                                if q is None:
                                    return 1
                                return q.dma_start(
                                    out=s_g[e, half],
                                    in_=a16t[:, o_gb[wp]:o_gb[wp] + 2048])

                            def f_wbu(q, e=e, half=half, wp=wp):
                                if q is None:
                                    return 1
                                return q.dma_start(
                                    out=s_u[e, half],
                                    in_=a16t[:, o_ub[wp]:o_ub[wp] + 2048])
                            wi = wb_i[0]
                            op_wbg = P.add("Q", f_wbg,
                                           deps=[opldg, opldu, wb_ring[wi % 4]],
                                           sem=sWB[wi % 4], name=f"B0wbg{e}.{half}")
                            wb_ring[wi % 4] = op_wbg
                            wb_i[0] += 1
                            wi = wb_i[0]
                            op_wbu = P.add("Q", f_wbu,
                                           deps=[opldu, wb_ring[wi % 4]],
                                           sem=sWB[wi % 4], name=f"B0wbu{e}.{half}")
                            wb_ring[wi % 4] = op_wbu
                            wb_i[0] += 1
                            wb_map[("g", e, half)] = op_wbg
                            wb_map[("u", e, half)] = op_wbu
                            prev[("wbgb", wp)] = op_wbg
                            prev[("wbub", wp)] = op_wbu
                        else:
                            dg = [bp, wbg_b, wb_map.get(("g", e, half))] + fen
                            du = [bp, wbu_b, wb_map.get(("u", e, half))] + fen
                            opldg = P.add("G", f_ldg, deps=dg, sem=sW[wp],
                                          name=f"B0ldg{tc}.{e}.{half}")
                            opldu = P.add("Q", f_ldu, deps=du, sem=sWu[wp],
                                          name=f"B0ldu{tc}.{e}.{half}")

                        for ici in range(2):
                            ic = half * 2 + ici
                            gp_ = ic % 2
                            up_ = 2 + ic % 2

                            def f_gmm(t, wp=wp, ici=ici, gp_=gp_, tc=tc):
                                for hc in range(NHC):
                                    li = t.matmul(
                                        pb[gp_][:, :],
                                        a16t[:, o_gb[wp] + hc * 256 + ici * 128:
                                             o_gb[wp] + hc * 256 + ici * 128 + 128],
                                        xn_sb[:, hc * T + tc * TC: hc * T + tc * TC + TC],
                                        start=(hc == 0), stop=(hc == NHC - 1))
                                return li
                            dg = [opldg, opldu, prev.get(("pg", gp_)), A["xn_last"]]
                            if tc > 0 and e == 0 and ic < 2:
                                dg.append(gevac_prev)
                            op_gmm = P.add("T", f_gmm, deps=dg, name=f"B0g{tc}.{e}.{ic}")

                            def f_umm(t, wp=wp, ici=ici, up_=up_, tc=tc):
                                for hc in range(NHC):
                                    li = t.matmul(
                                        pb[up_][:, :],
                                        a16t[:, o_ub[wp] + hc * 256 + ici * 128:
                                             o_ub[wp] + hc * 256 + ici * 128 + 128],
                                        xn_sb[:, hc * T + tc * TC: hc * T + tc * TC + TC],
                                        start=(hc == 0), stop=(hc == NHC - 1))
                                return li
                            du = [opldu, opldg, prev.get(("pu", up_)), A["xn_last"]]
                            if tc > 0 and e == 0 and ic < 2:
                                du.append(gevac_prev)
                            op_umm = P.add("T", f_umm, deps=du, name=f"B0u{tc}.{e}.{ic}")

                            def f_silu(s, ic=ic, gp_=gp_):
                                return s.activation(
                                    a32t[:, o_silu[ic % 2]:o_silu[ic % 2] + TC],
                                    pb[gp_][:, :], AF.Silu)
                            dsi = [op_gmm, prev.get(("h1p", ic % 2))]
                            op_si = P.add("S", f_silu, deps=dsi, name=f"B0si{tc}.{e}.{ic}")

                            if e < NE:
                                def f_h1(v, ic=ic, hp_e=hp_e, up_=up_, e=e):
                                    v.tensor_tensor(
                                        a32t[:, o_silu[ic % 2]:o_silu[ic % 2] + TC],
                                        a32t[:, o_silu[ic % 2]:o_silu[ic % 2] + TC],
                                        pb[up_][:, :], ALU.mult)
                                    v.drain()
                                    return v.tensor_tensor(
                                        a16t[:, hp_e + ic * TC: hp_e + (ic + 1) * TC],
                                        a32t[:, o_silu[ic % 2]:o_silu[ic % 2] + TC],
                                        pb[4 + e % 2][:, :], ALU.mult)
                                dh = [op_si, op_umm, prev.get(("cbc", e)), dn_last_prev]
                            else:
                                def f_h1(v, ic=ic, hp_e=hp_e, up_=up_):
                                    return v.tensor_tensor(
                                        a16t[:, hp_e + ic * TC: hp_e + (ic + 1) * TC],
                                        a32t[:, o_silu[ic % 2]:o_silu[ic % 2] + TC],
                                        pb[up_][:, :], ALU.mult)
                                dh = [op_si, op_umm, dn_last_prev]
                            op_h1 = P.add("V", f_h1, deps=dh, name=f"B0h{tc}.{e}.{ic}")
                            prev[("pg", gp_)] = op_h1
                            prev[("pu", up_)] = op_h1
                            prev[("h1p", ic % 2)] = op_h1
                            e_h1.append(op_h1)
                        prev[("gubuf", hidx)] = e_h1[-1]
                        hidx += 1

                    if e + 1 < NE:
                        def f_cbc(t, e=e, tc=tc):
                            return t.matmul(
                                pb[4 + (e + 1) % 2][:, :],
                                oneh_sb[0:16, (e + 1) * 128:(e + 2) * 128],
                                cfm_sb[0:16, tc * TC:(tc + 1) * TC],
                                start=True, stop=True)
                        dc = [A["c_last"], prev.get(("pcb", (e + 1) % 2))]
                        op_cbc = P.add("T", f_cbc, deps=dc, name=f"B0cbc{tc}.{e + 1}")
                        prev[("cbc", e + 1)] = op_cbc
                    if e < NE:
                        prev[("pcb", e % 2)] = e_h1[-1]
                    hlast.append(e_h1[-1])

                dn_ops = []
                for e in range(NXP):
                    wp = e % 2
                    if tc == 0:
                        dsrc = l0_d2[e] if e < NE else l0_sd2[e - NE]
                    else:
                        dsrc = s_d[e]

                    def f_ldd(q, dsrc=dsrc, wp=wp):
                        if q is None:
                            return 1
                        return q.dma_start(
                            out=a16t[:, o_db[wp]:o_db[wp] + 4096], in_=dsrc)
                    dld = [prev.get(("dnw", wp)), prev.get(("wbdb", wp))]
                    if tc == 0 and e < 2:
                        dld += ldfence
                    if tc == 0:
                        opldd = P.add("G", f_ldd, deps=dld,
                                      sem=sW[2 + wp], name=f"B0ldd{tc}.{e}")

                        def f_wbd(q, e=e, wp=wp):
                            if q is None:
                                return 1
                            return q.dma_start(
                                out=s_d[e],
                                in_=a16t[:, o_db[wp]:o_db[wp] + 4096])
                        wi = wb_i[0]
                        op_wbd = P.add("Q", f_wbd, deps=[opldd, wb_ring[wi % 4]],
                                       sem=sWB[wi % 4], name=f"B0wbd{e}")
                        wb_ring[wi % 4] = op_wbd
                        wb_i[0] += 1
                        wb_map[("d", e)] = op_wbd
                        prev[("wbdb", wp)] = op_wbd
                    else:
                        dld += [wb_map.get(("d", e))]
                        opldd = P.add("G" if wp == 0 else "Q", f_ldd, deps=dld,
                                      sem=(sW[2] if wp == 0 else sWd[1]),
                                      name=f"B0ldd{tc}.{e}")

                    def f_dn(t, e=e, wp=wp):
                        for hcc in range(NHC):
                            for ic in range(4):
                                li = t.matmul(
                                    pb[hcc][:, :],
                                    a16t[:, o_db[wp] + ic * 1024 + hcc * 128:
                                         o_db[wp] + ic * 1024 + hcc * 128 + 128],
                                    a16t[:, o_hp + e * 2048 + ic * TC:
                                         o_hp + e * 2048 + (ic + 1) * TC],
                                    start=(e == 0 and ic == 0),
                                    stop=(e == NXP - 1 and ic == 3))
                        return li
                    ddn = [opldd, hlast[e]]
                    if e == 0:
                        ddn += [hlast[-1] if False else hlast[0], gevac_prev]
                        ddn += hlast[:1]
                        ddn.append(hlast[NXP - 1])
                    op_dn = P.add("T", f_dn, deps=ddn, name=f"B0dn{tc}.{e}")
                    prev[("dnw", wp)] = op_dn
                    dn_ops.append(op_dn)
                dn_last_prev = dn_ops[-1]

                gev = []
                for hcc in range(NHC):
                    def f_gev(s, hcc=hcc, tc=tc):
                        return s.activation(
                            g_sb[:, hcc * T + tc * TC: hcc * T + tc * TC + TC],
                            pb[hcc][:, :], AF.Silu)
                    gev.append(P.add("S", f_gev, deps=[dn_ops[-1]],
                                     name=f"B0gev{tc}.{hcc}"))
                gevac_prev = gev[-1]

            a32.pop()
            a16.pop()
            return dict(g_last=gevac_prev, lasts=[gevac_prev, dn_last_prev])

        def phase_B1(A, ldfence):
            a16.push()
            o_gb = [a16.alloc(1024) for _ in range(2)]
            o_ub = [a16.alloc(1024) for _ in range(2)]
            o_db = [a16.alloc(3072) for _ in range(2)]
            o_hp = a16.alloc(NIC1 * 1024)
            a32.push()
            o_silu = [a32.alloc(TC) for _ in range(2)]

            prev = {}
            gevac_prev = None
            dmm_last = None
            for sc in range(2):
                h_all = []
                for ic in range(NIC1):
                    wp = ic % 2
                    gsrc = l1_g2[ic]
                    usrc = l1_u2[ic]

                    def f_ldg(q, gsrc=gsrc, wp=wp):
                        if q is None:
                            return 1
                        return q.dma_start(
                            out=a16t[:, o_gb[wp]:o_gb[wp] + 1024], in_=gsrc)

                    def f_ldu(q, usrc=usrc, wp=wp):
                        if q is None:
                            return 1
                        return q.dma_start(
                            out=a16t[:, o_ub[wp]:o_ub[wp] + 1024], in_=usrc)
                    gidx = sc * NIC1 + ic
                    bp = prev.get(("gubuf", gidx - 2))
                    fen = ldfence if gidx < 2 else []
                    opldg = P.add("G", f_ldg, deps=[bp] + fen, sem=sW[wp],
                                  name=f"B1ldg{sc}.{ic}")
                    opldu = P.add("G", f_ldu, deps=[bp] + fen, sem=sW[wp],
                                  name=f"B1ldu{sc}.{ic}")

                    h_ops = []
                    for tcc in range(2):
                        tcg = sc * 2 + tcc

                        def f_gmm(t, wp=wp, tcg=tcg, tcc=tcc):
                            for hc in range(NHC):
                                li = t.matmul(
                                    pb[tcc][:, :],
                                    a16t[:, o_gb[wp] + hc * 128: o_gb[wp] + (hc + 1) * 128],
                                    xn_sb[:, hc * T + tcg * TC: hc * T + tcg * TC + TC],
                                    start=(hc == 0), stop=(hc == NHC - 1))
                            return li
                        dg = [opldg, opldu, prev.get(("pg", tcc)), A["xn_last"]]
                        op_gmm = P.add("T", f_gmm, deps=dg, name=f"B1g{sc}.{ic}.{tcc}")

                        def f_umm(t, wp=wp, tcg=tcg, tcc=tcc):
                            for hc in range(NHC):
                                li = t.matmul(
                                    pb[2 + tcc][:, :],
                                    a16t[:, o_ub[wp] + hc * 128: o_ub[wp] + (hc + 1) * 128],
                                    xn_sb[:, hc * T + tcg * TC: hc * T + tcg * TC + TC],
                                    start=(hc == 0), stop=(hc == NHC - 1))
                            return li
                        du = [opldu, opldg, prev.get(("pu", tcc)), A["xn_last"]]
                        op_umm = P.add("T", f_umm, deps=du, name=f"B1u{sc}.{ic}.{tcc}")

                        def f_silu(s, tcc=tcc):
                            return s.activation(
                                a32t[:, o_silu[tcc]:o_silu[tcc] + TC],
                                pb[tcc][:, :], AF.Silu, scale=DQ)
                        op_si = P.add("S", f_silu, deps=[op_gmm, prev.get(("h1p", tcc))],
                                      name=f"B1si{sc}.{ic}.{tcc}")

                        def f_h(v, ic=ic, tcc=tcc):
                            return v.tensor_tensor(
                                a16t[:, o_hp + ic * 1024 + tcc * TC:
                                     o_hp + ic * 1024 + (tcc + 1) * TC],
                                a32t[:, o_silu[tcc]:o_silu[tcc] + TC],
                                pb[2 + tcc][:, :], ALU.mult)
                        dh = [op_si, op_umm]
                        if sc == 1 and ic < 2:
                            dh.append(dmm_last)
                        op_h = P.add("V", f_h, deps=dh, name=f"B1h{sc}.{ic}.{tcc}")
                        prev[("pg", tcc)] = op_h
                        prev[("pu", tcc)] = op_h
                        prev[("h1p", tcc)] = op_h
                        h_ops.append(op_h)
                    prev[("gubuf", gidx)] = h_ops[-1]
                    h_all += h_ops

                for hcc in range(NHC):
                    wp = hcc % 2
                    dsrc = l1_d2[hcc]

                    def f_ldd(q, dsrc=dsrc, wp=wp):
                        if q is None:
                            return 1
                        return q.dma_start(
                            out=a16t[:, o_db[wp]:o_db[wp] + 3072], in_=dsrc)
                    dld1 = [prev.get(("dnw", wp))]
                    if sc == 0 and hcc < 2:
                        dld1 += ldfence
                    opldd = P.add("G", f_ldd, deps=dld1,
                                  sem=sW[2 + wp], name=f"B1ldd{sc}.{hcc}")

                    for tcc in range(2):
                        tcg = sc * 2 + tcc

                        def f_dmm(t, wp=wp, tcc=tcc):
                            for ic in range(NIC1):
                                li = t.matmul(
                                    pb[4 + tcc][:, :],
                                    a16t[:, o_db[wp] + ic * 128: o_db[wp] + (ic + 1) * 128],
                                    a16t[:, o_hp + ic * 1024 + tcc * TC:
                                         o_hp + ic * 1024 + (tcc + 1) * TC],
                                    start=(ic == 0), stop=(ic == NIC1 - 1))
                            return li
                        ddm = [opldd, h_all[-1], prev.get(("gev", tcc))]
                        op_dmm = P.add("T", f_dmm, deps=ddm, name=f"B1d{sc}.{hcc}.{tcc}")
                        prev[("dnw", wp)] = op_dmm
                        dmm_last = op_dmm

                        def f_gev(s, hcc=hcc, tcg=tcg, tcc=tcc):
                            return s.activation(
                                g_sb[:, hcc * T + tcg * TC: hcc * T + tcg * TC + TC],
                                pb[4 + tcc][:, :], AF.Silu, scale=DQ * DQ)
                        op_gev = P.add("S", f_gev, deps=[op_dmm],
                                       name=f"B1gev{sc}.{hcc}.{tcc}")
                        prev[("gev", tcc)] = op_gev
                        gevac_prev = op_gev
            a32.pop()
            a16.pop()
            return dict(g_last=gevac_prev, lasts=[gevac_prev, dmm_last])

        def phase_C(L, A, B, l0_stores=None):
            nkv = 2 if L == 0 else 8
            wk_src = l0_wk2 if L == 0 else l1_wk2
            wv_src = l0_wv2 if L == 0 else l1_wv2
            wq_src = l0_wq2 if L == 0 else l1_wq2
            wo_src = l0_wo2 if L == 0 else l1_wo2
            vwid = 256 if L == 0 else 1024
            nvh = vwid // HD
            xsrc = x_in if L == 0 else x_res
            xdst = x_res if L == 0 else out_t

            a16.push()
            o_kfm = a16.alloc(nkv * T)
            o_vtm = a16.alloc(16 * 65 * nvh)
            o_qa = [a16.alloc(TC) for _ in range(2)]
            o_qb = [a16.alloc(TC) for _ in range(2)]
            o_qt = a16.alloc(TC) if L == 0 else None
            o_bvb = a16.alloc(1024) if L == 1 else None
            o_pt = [a16.alloc(2 * TC) for _ in range(3)]
            o_sh = a16.alloc(2 * NHC * TC)
            o_wv = o_sh
            o_oall = [o_sh, o_sh + NHC * TC]
            o_wq = [a16.alloc(1024) for _ in range(2)]
            o_wo = [a16.alloc(1024) for _ in range(2)]
            o_nrm = A16 - 1024
            o_cs = a16.alloc(2 * T) if L == 0 else None
            o_cm = a16.alloc(4 * TC) if L == 1 else None
            a32.push()
            o_sums = a32.alloc(2 * TC)
            o_rcf = a32.alloc(2 * TC)
            o_rt = a32.alloc(2 * TC)
            o_xin = [a32.alloc(TC) for _ in range(2)]
            o_xout = [a32.alloc(TC) for _ in range(2)]

            fence = B["lasts"] + [A["xn_last"]]
            pv = {}
            ops = {}

            if L == 0:
                def f_ldcs(q):
                    return 1 if q is None else q.dma_start(
                        out=a16t[:, o_cs:o_cs + 2 * T], in_=cs2[:, :])
                op_cs = P.add("G", f_ldcs, deps=fence, sem=sCS, name="C0ldcs")
            else:
                def f_ldcm(q):
                    return 1 if q is None else q.dma_start(
                        out=a16t[:, o_cm:o_cm + 4 * TC], in_=cmask[:, :])
                op_cs = P.add("G", f_ldcm, deps=fence, sem=sCS, name="C1ldcm")

            def f_msets(v):
                for tt in range(16):
                    v.memset(
                        a16t[:, o_vtm + tt * 65 * nvh:
                             o_vtm + (tt + 1) * 65 * nvh].rearrange(
                            "p (h w) -> p h w", w=65)[:, :, 64:65], 1.0)
                for i in range(2):
                    v.memset(a16t[64:128, o_qa[i]:o_qa[i] + TC], 0.0)
                    v.memset(a16t[0:64, o_qb[i]:o_qb[i] + TC], 0.0)
                return v.memset(a16t[:, o_nrm:o_nrm + 2 * TC], 0.0)
            op_ms = P.add("V", f_msets, deps=fence, name=f"C{L}msets")

            if L == 1:
                def f_ldbv(g):
                    return 1 if g is None else g.dma_start(
                        out=a16t[:, o_bvb:o_bvb + 1024], in_=l1_bvbc[:, :])
                op_bv = P.add("G", f_ldbv, deps=fence, sem=sCS, name="C1ldbv")

            kv_last = None
            for c in range(nkv):
                wp = c % 2
                ksrc = wk_src[c]

                def f_ldwk(q, ksrc=ksrc, wp=wp):
                    if q is None:
                        return 1
                    return q.dma_start(
                        out=a16t[:, o_wq[wp]:o_wq[wp] + 1024], in_=ksrc)
                opldk = P.add("G", f_ldwk, deps=[pv.get(("kbuf", c - 2))] + fence,
                              sem=sKW[c % 2], name=f"C{L}ldk{c}")

                for tc in range(NTC):
                    def f_kmm(t, wp=wp, tc=tc):
                        for hc in range(NHC):
                            li = t.matmul(
                                pb6[:, :],
                                a16t[:, o_wq[wp] + hc * 128: o_wq[wp] + (hc + 1) * 128],
                                xn_sb[:, hc * T + tc * TC: hc * T + tc * TC + TC],
                                start=(hc == 0), stop=(hc == NHC - 1))
                        return li
                    dk = [opldk, A["xn_last"], pv.get(("p6",))]
                    op_kmm = P.add("T", f_kmm, deps=dk, name=f"C{L}kmm{c}.{tc}")
                    pv[("kbuf", c)] = op_kmm

                    if L == 0:
                        def f_kev(v, c=c, tc=tc):
                            return v.tensor_copy(a16t[:, o_qt:o_qt + TC], pb6[:, :])
                        op_kev = P.add("V", f_kev,
                                       deps=[op_kmm, pv.get(("qtT",))],
                                       name=f"C0kev{c}.{tc}")
                        pv[("p6",)] = op_kev

                        def f_krot(t, c=c, tc=tc):
                            return t.matmul(pb7[:, :], cst16[:, B_R2T:B_R2T + 128],
                                            a16t[:, o_qt:o_qt + TC],
                                            start=True, stop=True)
                        op_krot = P.add("T", f_krot, deps=[op_kev, pv.get(("p7",))],
                                        name=f"C0krot{c}.{tc}")
                        pv[("qtT",)] = op_krot

                        def f_krope(v, c=c, tc=tc):
                            v.tensor_tensor(a32t[:, o_rt:o_rt + TC], pb7[:, :],
                                            a16t[:, o_cs + T + tc * TC:
                                                 o_cs + T + (tc + 1) * TC], ALU.mult)
                            v.drain()
                            v.tensor_tensor(a32t[:, o_rt + TC:o_rt + 2 * TC],
                                            a16t[:, o_qt:o_qt + TC],
                                            a16t[:, o_cs + tc * TC: o_cs + (tc + 1) * TC],
                                            ALU.mult)
                            v.drain()
                            return v.tensor_tensor(
                                a16t[:, o_kfm + c * T + tc * TC:
                                     o_kfm + c * T + (tc + 1) * TC],
                                a32t[:, o_rt:o_rt + TC],
                                a32t[:, o_rt + TC:o_rt + 2 * TC], ALU.add)
                        op_kr = P.add("V", f_krope, deps=[op_krot, op_cs],
                                      name=f"C0krope{c}.{tc}")
                        pv[("p7",)] = op_kr
                        kv_last = op_kr
                    else:
                        def f_kev(v, c=c, tc=tc):
                            return v.tensor_scalar(
                                a16t[:, o_kfm + c * T + tc * TC:
                                     o_kfm + c * T + (tc + 1) * TC],
                                pb6[:, :],
                                cst32[:, C_BQK + 8 + c: C_BQK + 9 + c],
                                None, ALU.add)
                        op_kev = P.add("V", f_kev, deps=[op_kmm], name=f"C1kev{c}.{tc}")
                        pv[("p6",)] = op_kev
                        kv_last = op_kev
            op_kevl = pv[("p6",)]

            def f_ldwv(q):
                if q is None:
                    return 1
                return q.dma_start(
                    out=a16t[:, o_wv:o_wv + NHC * vwid], in_=wv_src)
            opldv = P.add("G", f_ldwv, deps=fence, sem=sVW, name=f"C{L}ldv")

            nhalf = max(1, vwid // 512)
            wvw = min(512, vwid)
            v_last = None
            op_vmm_last = None
            for tt in range(16):
                for hf in range(nhalf):
                    w0 = hf * 512

                    def f_vmm(t, tt=tt, w0=w0):
                        for hc in range(NHC):
                            li = t.matmul(
                                pb7[0:128, 0:wvw],
                                xn_sb[:, hc * T + tt * 128: hc * T + (tt + 1) * 128],
                                a16t[:, o_wv + hc * vwid + w0:
                                     o_wv + hc * vwid + w0 + wvw],
                                start=(hc == 0), stop=(hc == NHC - 1))
                        return li
                    dvm = [opldv, A["xn_last"], pv.get(("p7",))]
                    op_vmm = P.add("T", f_vmm, deps=dvm, name=f"C{L}vmm{tt}.{hf}")
                    op_vmm_last = op_vmm

                    def f_vev(v, tt=tt, w0=w0):
                        nh_here = wvw // HD
                        dst = a16t[:, o_vtm + tt * 65 * nvh + (w0 // HD) * 65:
                                   o_vtm + tt * 65 * nvh + (w0 // HD) * 65 + nh_here * 65]
                        dst3 = dst.rearrange("p (h w) -> p h w", w=65)[:, :, 0:64]
                        src3 = pb7[0:128, 0:wvw].rearrange("p (h w) -> p h w", w=64)
                        if L == 1:
                            return v.tensor_tensor(
                                dst3, src3,
                                a16t[:, o_bvb + w0:o_bvb + w0 + wvw].rearrange(
                                    "p (h w) -> p h w", w=64), ALU.add)
                        return v.tensor_copy(dst3, src3)
                    dve = [op_vmm, op_ms]
                    if L == 1:
                        dve.append(op_bv)
                    op_vev = P.add("V", f_vev, deps=dve,
                                   name=f"C{L}vev{tt}.{hf}")
                    pv[("p7",)] = op_vev
                    v_last = op_vev
            op_vevl = v_last

            pend = []
            curg = [None, []]

            def pbo_of(fc):
                return pb6 if fc % 2 == 0 else pb5

            def make_oproj_units(qcd, norm_deps):
                groups = []
                for fc in range(NHC):
                    wp = fc % 2
                    xp = fc % 2
                    units = []

                    def u_ld(fc=fc, wp=wp, qcd=qcd):
                        def f_ldwo(q, fc=fc, wp=wp):
                            if q is None:
                                return 1
                            return q.dma_start(
                                out=a16t[:, o_wo[wp]:o_wo[wp] + 1024],
                                in_=wo_src[fc])
                        prv = ops.get(("opj", fc - 2, 7))
                        if prv is None:
                            prv = ops.get(("opj", fc + 6, 7))
                        dl = [prv] if prv is not None else list(fence)
                        ops[("ldwo", fc)] = P.add("G", f_ldwo, deps=dl,
                                                  sem=sWO[fc % 2],
                                                  name=f"C{L}ldwo{qcd}.{fc}")
                    units.append(u_ld)
                    for p2 in range(8):
                        def u_mm(fc=fc, p2=p2, wp=wp, qcd=qcd):
                            def f_mm(t, fc=fc, p2=p2, wp=wp, qcd=qcd):
                                return t.matmul(
                                    pbo_of(fc)[:, :],
                                    a16t[:, o_wo[wp] + p2 * 128:
                                         o_wo[wp] + p2 * 128 + 128],
                                    a16t[:, o_oall[qcd % 2] + p2 * TC:
                                         o_oall[qcd % 2] + (p2 + 1) * TC],
                                    start=(p2 == 0), stop=(p2 == 7))
                            dm = [ops[("ldwo", fc)]]
                            if p2 == 0:
                                prvb = ops.get(("fuse", fc - 2))
                                if prvb is None:
                                    prvb = ops.get(("fuse", fc + 6))
                                dm += [prvb] + norm_deps
                                dm += pv.get(("normd", 1 - fc % 2), [])
                                dm.append(pv.get(("rcps", 1 - fc % 2)))
                                if fc == 0:
                                    dm.append(op_kevl)
                                if fc == 1:
                                    dm.append(op_vevl)
                            ops[("opj", fc, p2)] = P.add(
                                "T", f_mm, deps=dm, name=f"C{L}op{qcd}.{fc}.{p2}")
                            ops[("opjlast",)] = ops[("opj", fc, p2)]
                        units.append(u_mm)

                    def u_fin(fc=fc, xp=xp, qcd=qcd):
                        def f_ldx(q, qcd=qcd, fc=fc, xp=xp):
                            if q is None:
                                return 1
                            return q.dma_start(
                                out=a32t[:, o_xin[xp]:o_xin[xp] + TC],
                                in_=xsrc[fc * 128:(fc + 1) * 128,
                                         qcd * TC:(qcd + 1) * TC])
                        prvf = ops.get(("fuse", fc - 2))
                        if prvf is None:
                            prvf = ops.get(("fuse", fc + 6))
                        dlx = [prvf]
                        if L == 1 and qcd == 0 and fc < 2:
                            dlx += l0_stores
                        if qcd == 0 and fc < 2:
                            dlx += fence
                        op_ldx = P.add("Q", f_ldx, deps=dlx, sem=sXl[xp],
                                       name=f"C{L}ldx{qcd}.{fc}")

                        def f_fuse(v, fc=fc, qcd=qcd, xp=xp, L=L):
                            bias_ap = (cst32[:, C_BOUT + fc: C_BOUT + fc + 1]
                                       if L == 1 else 0.0)
                            v.scalar_tensor_tensor(
                                a32t[:, o_xout[xp]:o_xout[xp] + TC],
                                pbo_of(fc)[:, :], bias_ap,
                                g_sb[:, fc * T + qcd * TC: fc * T + qcd * TC + TC],
                                ALU.add, ALU.mult)
                            v.drain()
                            return v.tensor_tensor(
                                a32t[:, o_xout[xp]:o_xout[xp] + TC],
                                a32t[:, o_xout[xp]:o_xout[xp] + TC],
                                a32t[:, o_xin[xp]:o_xin[xp] + TC], ALU.add)
                        dfu = [ops[("opj", fc, 7)], op_ldx, B["g_last"],
                               ops.get(("stx", xp))]
                        op_fuse = P.add("V", f_fuse, deps=dfu,
                                        name=f"C{L}fu{qcd}.{fc}")
                        ops[("fuse", fc)] = op_fuse
                        ops[("fusel",)] = op_fuse
                        pv[("lastfuse",)] = op_fuse

                        def f_stx(q, qcd=qcd, fc=fc, xp=xp):
                            if q is None:
                                return 1
                            return q.dma_start(
                                out=xdst[fc * 128:(fc + 1) * 128,
                                         qcd * TC:(qcd + 1) * TC],
                                in_=a32t[:, o_xout[xp]:o_xout[xp] + TC])
                        ops[("stx", xp)] = P.add("Q", f_stx, deps=[op_fuse],
                                                 sem=(sO if xp == 0 else sO2),
                                                 name=f"C{L}stx{qcd}.{fc}")
                    units.append(u_fin)
                    groups.append(units)
                return groups

            def start_group():
                if curg[0] is None and pend:
                    g = pend.pop(0)
                    g[0]()
                    curg[0] = g
                    curg[1] = list(g[1:-1])

            def emit_unit():
                if curg[0] is not None and curg[1]:
                    curg[1].pop(0)()

            def finish_group():
                if curg[0] is not None:
                    while curg[1]:
                        curg[1].pop(0)()
                    curg[0][-1]()
                    curg[0] = None

            epi_pend = [None]
            epi1_pend = [None]

            def flush_epi():
                if epi_pend[0] is not None:
                    epi_pend[0]()
                    epi_pend[0] = None

            def flush_epi1():
                if epi1_pend[0] is not None:
                    epi1_pend[0]()
                    epi1_pend[0] = None

            for qc in range(NTC):
                pv[("oallfree",)] = ops.get(("opjlast",))
                nkc = 16 if L == 0 else 4 * qc + 4
                for p in range(8):
                    wp = p % 2
                    qi = qc * 8 + p
                    pi = qi % 2
                    av_a, av_b = (pb4, pb5) if pi == 0 else (pb6, pb7)
                    qsrc = wq_src[p]

                    def f_ldwq(q, qsrc=qsrc, wp=wp):
                        if q is None:
                            return 1
                        return q.dma_start(
                            out=a16t[:, o_wq[wp]:o_wq[wp] + 1024], in_=qsrc)
                    dlq = [pv.get(("qbuf", qi - 2))]
                    if qc == 0 and p < 2:
                        dlq.append(pv.get(("kbuf", nkv - 2 + p)))
                    opldq = P.add("G", f_ldwq, deps=dlq, sem=sQW[p % 2],
                                  name=f"C{L}ldq{qc}.{p}")

                    def f_qmm(t, wp=wp, qc=qc):
                        for hc in range(NHC):
                            li = t.matmul(
                                pst1[:, 0:512],
                                a16t[:, o_wq[wp] + hc * 128: o_wq[wp] + (hc + 1) * 128],
                                xn_sb[:, hc * T + qc * TC: hc * T + qc * TC + TC],
                                start=(hc == 0), stop=(hc == NHC - 1))
                        return li
                    dq = [opldq, A["xn_last"], pv.get(("exp", 1))]
                    if qi == 0:
                        dq += fence + [pv.get(("kbuf", nkv - 1))]
                    op_qmm = P.add("T", f_qmm, deps=dq, name=f"C{L}qmm{qc}.{p}")
                    pv[("qbuf", qi)] = op_qmm

                    if L == 0:
                        def f_qev(v):
                            return v.tensor_copy(a16t[:, o_qt:o_qt + TC],
                                                 pst1[:, 0:512])
                        op_qev = P.add("V", f_qev, deps=[op_qmm, pv.get(("qtT",))],
                                       name=f"C0qev{qc}.{p}")

                        def f_qrot(t):
                            return t.matmul(pst1[:, 512:1024],
                                            cst16[:, B_R2T:B_R2T + 128],
                                            a16t[:, o_qt:o_qt + TC],
                                            start=True, stop=True)
                        op_qrot = P.add("T", f_qrot,
                                        deps=[op_qev, pv.get(("exp", 1))],
                                        name=f"C0qrot{qc}.{p}")
                        pv[("qtT",)] = op_qrot

                        def f_qrope(v, qc=qc, wp=wp):
                            v.tensor_tensor(a32t[:, o_rt:o_rt + TC],
                                            pst1[:, 512:1024],
                                            a16t[:, o_cs + T + qc * TC:
                                                 o_cs + T + (qc + 1) * TC], ALU.mult)
                            v.drain()
                            v.tensor_tensor(a32t[:, o_rt + TC:o_rt + 2 * TC],
                                            a16t[:, o_qt:o_qt + TC],
                                            a16t[:, o_cs + qc * TC:
                                                 o_cs + (qc + 1) * TC], ALU.mult)
                            v.drain()
                            v.tensor_tensor(a16t[0:64, o_qa[wp]:o_qa[wp] + TC],
                                            a32t[0:64, o_rt:o_rt + TC],
                                            a32t[0:64, o_rt + TC:o_rt + 2 * TC],
                                            ALU.add)
                            v.drain()
                            return v.tensor_tensor(
                                a16t[64:128, o_qb[wp]:o_qb[wp] + TC],
                                a32t[64:128, o_rt:o_rt + TC],
                                a32t[64:128, o_rt + TC:o_rt + 2 * TC], ALU.add)
                        dqr = [op_qrot, op_cs, op_ms, pv.get(("stl", wp))]
                        q_ready = P.add("V", f_qrope, deps=dqr,
                                        name=f"C0qrope{qc}.{p}")
                    else:
                        def f_qev(v, p=p, wp=wp):
                            v.tensor_scalar(a16t[0:64, o_qa[wp]:o_qa[wp] + TC],
                                            pst1[0:64, 0:512],
                                            cst32[0:64, C_BQK + p: C_BQK + p + 1],
                                            None, ALU.add)
                            v.drain()
                            return v.tensor_scalar(
                                a16t[64:128, o_qb[wp]:o_qb[wp] + TC],
                                pst1[64:128, 0:512],
                                cst32[64:128, C_BQK + p: C_BQK + p + 1],
                                None, ALU.add)
                        dqr = [op_qmm, op_ms, pv.get(("stl", wp))]
                        q_ready = P.add("V", f_qev, deps=dqr,
                                        name=f"C1qev{qc}.{p}")

                    if L == 0:
                        kch = p // 4
                        kva, kvb = 2 * (p // 4), 2 * (p // 4) + 1
                    else:
                        kch = p
                        kva, kvb = 2 * p, 2 * p + 1

                    flush_epi1()
                    start_group()
                    exp_ops = {}

                    def emit_av(kc, kva=kva, kvb=kvb, nkc=nkc, qc=qc, p=p,
                                av_a=av_a, av_b=av_b, pi=pi, qi=qi):
                        kp4 = kc % 2
                        pt_d = exp_ops[kc][1]

                        def f_av(t, kc=kc, kp4=kp4, kva=kva, kvb=kvb, nkc=nkc,
                                 av_a=av_a, av_b=av_b):
                            t.matmul(av_a[0:65, :],
                                     a16t[:, o_vtm + kc * 65 * nvh + kva * 65:
                                          o_vtm + kc * 65 * nvh + kva * 65 + 65],
                                     a16t[:, o_pt[kp4]:o_pt[kp4] + TC],
                                     start=(kc == 0), stop=(kc == nkc - 1))
                            return t.matmul(
                                av_b[0:65, :],
                                a16t[:, o_vtm + kc * 65 * nvh + kvb * 65:
                                     o_vtm + kc * 65 * nvh + kvb * 65 + 65],
                                a16t[:, o_pt[kp4] + TC:o_pt[kp4] + 2 * TC],
                                start=(kc == 0), stop=(kc == nkc - 1))
                        dav = [pt_d]
                        if kc == 0:
                            dav += [v_last, op_ms, pv.get(("lastfuse",))]
                            dav += pv.get(("normd", pi), [])
                            if qi == 1:
                                dav += [op_kevl, op_vevl]
                        op = P.add("T", f_av, deps=dav, name=f"C{L}av{qc}.{p}.{kc}")
                        pv[("avpt", kp4)] = op
                        return op

                    op_st = None
                    for kc in range(nkc):
                        kp = kc % 2

                        def f_st(t, kc=kc, kp=kp, kch=kch, wp=wp):
                            t.matmul(pst[kp][:, 0:512],
                                     a16t[:, o_kfm + kch * T + kc * 128:
                                          o_kfm + kch * T + kc * 128 + 128],
                                     a16t[:, o_qa[wp]:o_qa[wp] + TC],
                                     start=True, stop=True)
                            return t.matmul(
                                pst[kp][:, 512:1024],
                                a16t[:, o_kfm + kch * T + kc * 128:
                                     o_kfm + kch * T + kc * 128 + 128],
                                a16t[:, o_qb[wp]:o_qb[wp] + TC],
                                start=True, stop=True)
                        dst_ = [q_ready, kv_last]
                        if kc >= 2:
                            dst_.append(exp_ops[kc - 2][0])
                        if kc == 0:
                            dst_.append(pv.get(("exp", 0)))
                        op_st = P.add("T", f_st, deps=dst_,
                                      name=f"C{L}st{qc}.{p}.{kc}")
                        if kc == nkc - 1:
                            pv[("stl", wp)] = op_st

                        if kc == 1:
                            flush_epi()
                            start_group()
                        if kc >= 1:
                            emit_av(kc - 1)
                        if kc >= 6:
                            emit_unit()
                            if kc >= nkc - 3:
                                emit_unit()

                        kp4 = kc % 2

                        def f_exp(s, kc=kc, kp=kp, kp4=kp4):
                            return s.activation(
                                a16t[:, o_pt[kp4]:o_pt[kp4] + 2 * TC],
                                pst[kp][:, 0:1024], AF.Exp, scale=0.125)
                        de = [op_st, pv.get(("avpt", kp4))]
                        op_exp = P.add("S", f_exp, deps=de,
                                       name=f"C{L}exp{qc}.{p}.{kc}")
                        pt_dep = op_exp

                        if L == 1 and kc >= 4 * qc:
                            j = kc - 4 * qc

                            def f_msk(v, kp4=kp4, j=j):
                                v.tensor_tensor(a16t[:, o_pt[kp4]:o_pt[kp4] + TC],
                                                a16t[:, o_pt[kp4]:o_pt[kp4] + TC],
                                                a16t[:, o_cm + j * TC:
                                                     o_cm + (j + 1) * TC], ALU.mult)
                                v.drain()
                                return v.tensor_tensor(
                                    a16t[:, o_pt[kp4] + TC:o_pt[kp4] + 2 * TC],
                                    a16t[:, o_pt[kp4] + TC:o_pt[kp4] + 2 * TC],
                                    a16t[:, o_cm + j * TC: o_cm + (j + 1) * TC],
                                    ALU.mult)
                            pt_dep = P.add("V", f_msk, deps=[op_exp, op_cs],
                                           name=f"C1msk{qc}.{p}.{kc}")
                        exp_ops[kc] = (op_exp, pt_dep)

                    op_avl = emit_av(nkc - 1)
                    finish_group()
                    pv[("exp", 0)] = exp_ops[nkc - 2][0]
                    pv[("exp", 1)] = exp_ops[nkc - 1][0]

                    blk = o_oall[qc % 2] + p * TC

                    def mk_epi1(op_avl=op_avl, blk=blk, qc=qc, p=p,
                                av_a=av_a, av_b=av_b, pi=pi):
                      def emit1():
                        def f_oeva(v, blk=blk):
                            return v.tensor_copy(a16t[0:64, blk:blk + TC],
                                                 av_a[0:64, :])
                        dea = [op_avl, pv.get(("oallfree",))]
                        if qc <= 1 and p == 0:
                            dea.append(op_vmm_last)
                        op_oeva = P.add("V", f_oeva, deps=dea, name=f"C{L}oea{qc}.{p}")

                        def f_oevb(v, blk=blk):
                            return v.tensor_copy(a16t[64:128, blk:blk + TC],
                                                 av_b[0:64, :])
                        op_oevb = P.add("V", f_oevb, deps=[op_avl],
                                        name=f"C{L}oeb{qc}.{p}")

                        def f_rcp(s):
                            s.activation(a32t[0:1, o_rcf:o_rcf + TC],
                                         av_a[64:65, :], AF.Ln)
                            s.drain()
                            s.activation(a32t[0:1, o_rcf + TC:o_rcf + 2 * TC],
                                         av_b[64:65, :], AF.Ln)
                            s.drain()
                            return s.activation(a16t[0:1, o_nrm:o_nrm + 2 * TC],
                                                a32t[0:1, o_rcf:o_rcf + 2 * TC],
                                                AF.Exp, scale=-1.0)
                        op_rcp = P.add("S", f_rcp, deps=[op_avl, pv.get(("bcd",))],
                                       name=f"C{L}rcp{qc}.{p}")
                        op_sums = op_rcp
                        pv[("rcps", pi)] = op_rcp

                        def mk_epi(op_rcp=op_rcp, blk=blk, qc=qc, p=p,
                                   av_a=av_a, av_b=av_b, pi=pi,
                                   pbread=(op_oeva, op_oevb, op_sums)):
                            def emit():
                                def f_bc(t):
                                    t.matmul(av_a[:, :], cst16[:, B_BR0:B_BR0 + 128],
                                             a16t[:, o_nrm:o_nrm + TC],
                                             start=True, stop=True)
                                    return t.matmul(
                                        av_b[:, :], cst16[:, B_BR0:B_BR0 + 128],
                                        a16t[:, o_nrm + TC:o_nrm + 2 * TC],
                                        start=True, stop=True)
                                op_bc = P.add("T", f_bc,
                                              deps=[op_rcp] + list(pbread),
                                              name=f"C{L}bc{qc}.{p}")
                                pv[("bcd",)] = op_bc

                                def f_na(v, blk=blk):
                                    return v.tensor_tensor(
                                        a16t[0:64, blk:blk + TC],
                                        a16t[0:64, blk:blk + TC],
                                        av_a[0:64, :], ALU.mult)
                                op_na = P.add("V", f_na, deps=[op_bc],
                                              name=f"C{L}na{qc}.{p}")

                                def f_nb(v, blk=blk):
                                    return v.tensor_tensor(
                                        a16t[64:128, blk:blk + TC],
                                        a16t[64:128, blk:blk + TC],
                                        av_b[64:128, :], ALU.mult)
                                op_nb = P.add("V", f_nb, deps=[op_bc],
                                              name=f"C{L}nb{qc}.{p}")
                                pv[("normd", pi)] = [op_na, op_nb]
                                if p == 7:
                                    pend.extend(make_oproj_units(qc, [op_na, op_nb]))
                            return emit
                        epi_pend[0] = mk_epi()
                      return emit1
                    epi1_pend[0] = mk_epi1()

                while pend or curg[0] is not None:
                    start_group()
                    finish_group()

            flush_epi1()
            flush_epi()
            while pend or curg[0] is not None:
                start_group()
                finish_group()

            stores = [ops[("stx", 0)], ops[("stx", 1)]]
            res = dict(stores=stores,
                       lasts=[ops[("opjlast",)], ops[("fusel",)]] + stores)
            a32.pop()
            a16.pop()
            return res


        a16.push()
        o_cfm = a16.alloc(T)
        cfm_sb = a16t[:, o_cfm:o_cfm + T]

        A0 = phase_A(0, fence0)
        if debug and d_xn[0] is not None:
            def f_dxn0(q):
                if q is None:
                    return 1
                return q.dma_start(out=d_xn[0].rearrange("(c p) t -> p c t", p=128),
                                   in_=xn_sb[:, :].rearrange("p (c t) -> p c t", t=T))
            dbg_prev[0] = P.add("G", f_dxn0, deps=[A0["xn_last"], dbg_prev[0]],
                                sem=sDbg, name="d_xn0")
            A0["lasts"].append(dbg_prev[0])

            def f_dC(q):
                if q is None:
                    return 1
                return q.dma_start(out=d_C[:, :], in_=cfm_sb[0:16, :])
            dbg_prev[0] = P.add("G", f_dC, deps=[A0["c_last"], dbg_prev[0]],
                                sem=sDbg, name="d_C0")
            A0["lasts"].append(dbg_prev[0])

        B0 = phase_B0(A0, fence0)
        if debug and d_g[0] is not None:
            def f_dg0(q):
                if q is None:
                    return 1
                return q.dma_start(out=d_g[0].rearrange("(c p) t -> p c t", p=128),
                                   in_=g_sb[:, :].rearrange("p (c t) -> p c t", t=T))
            dbg_prev[0] = P.add("G", f_dg0, deps=[B0["g_last"], dbg_prev[0]],
                                sem=sDbg, name="d_g0")
            B0["lasts"].append(dbg_prev[0])
        a16.pop()

        C0 = phase_C(0, A0, B0)

        A1 = phase_A(1, C0["lasts"])
        if debug and d_xn[1] is not None:
            def f_dxn1(q):
                if q is None:
                    return 1
                return q.dma_start(out=d_xn[1].rearrange("(c p) t -> p c t", p=128),
                                   in_=xn_sb[:, :].rearrange("p (c t) -> p c t", t=T))
            dbg_prev[0] = P.add("G", f_dxn1, deps=[A1["xn_last"], dbg_prev[0]],
                                sem=sDbg, name="d_xn1")
            A1["lasts"].append(dbg_prev[0])

        B1 = phase_B1(A1, C0["lasts"])
        if debug and d_g[1] is not None:
            def f_dg1(q):
                if q is None:
                    return 1
                return q.dma_start(out=d_g[1].rearrange("(c p) t -> p c t", p=128),
                                   in_=g_sb[:, :].rearrange("p (c t) -> p c t", t=T))
            dbg_prev[0] = P.add("G", f_dg1, deps=[B1["g_last"], dbg_prev[0]],
                                sem=sDbg, name="d_g1")
            B1["lasts"].append(dbg_prev[0])

        C1 = phase_C(1, A1, B1, l0_stores=C0["stores"])

        q_sems = sXa + sWu + sWB + [sWd[1], sXb[0], sXb[1], sXl[0], sXl[1], sO, sO2, sDbq]
        g_sems = [sCst, sDbg, sCS, sVW] + sWO + sW + sKW + sQW

        def f_qfin(q):
            if q is None:
                return 0
            for sm in q_sems:
                if sm.v > 0:
                    q.wait_ge(sm.h, sm.v)
            return None

        def f_gfin(g):
            if g is None:
                return 0
            for sm in g_sems:
                if sm.v > 0:
                    g.wait_ge(sm.h, sm.v)
            return None
        P.add("Q", f_qfin, deps=C1["stores"], sem=sO, name="final_wait_q")
        P.add("G", f_gfin, deps=[], sem=sCst, name="final_wait_g")

        P.emit(block)

    return nc, dbg


def _host_inputs(inputs):
    f32 = np.float32
    bf = ml_dtypes.bfloat16
    x = np.asarray(inputs["L0_attn_x"], f32)

    qperm = np.empty(D, np.int64)
    for j in range(D):
        p, s, d = j // 128, (j % 128) // 64, j % 64
        qperm[j] = L0_PAIRS[p][s] * 64 + d
    wq_r = np.ascontiguousarray(np.asarray(inputs["L0_attn_Wq"], f32)[:, qperm])
    wo_r = np.ascontiguousarray(np.asarray(inputs["L0_attn_Wo"], f32)[qperm, :])

    gT = np.transpose(np.asarray(inputs["L0_ffn_gate_w"], f32), (0, 2, 1))
    uT = np.transpose(np.asarray(inputs["L0_ffn_up_w"], f32), (0, 2, 1))
    dT = np.transpose(np.asarray(inputs["L0_ffn_down_w"], f32), (0, 2, 1))
    sgT = np.asarray(inputs["L0_ffn_sh_gate_w"], f32).T
    suT = np.asarray(inputs["L0_ffn_sh_up_w"], f32).T
    sdT = np.asarray(inputs["L0_ffn_sh_down_w"], f32).T
    g2 = np.ascontiguousarray(gT.reshape(16, 8, 128, 2, 256)
                              .transpose(0, 3, 2, 1, 4).reshape(16, 2, 128, 2048))
    u2 = np.ascontiguousarray(uT.reshape(16, 8, 128, 2, 256)
                              .transpose(0, 3, 2, 1, 4).reshape(16, 2, 128, 2048))
    d2 = np.ascontiguousarray(dT.reshape(16, 4, 128, 1024)
                              .transpose(0, 2, 1, 3).reshape(16, 128, 4096))
    sg2 = np.ascontiguousarray(sgT.reshape(8, 128, 2, 2, 256)
                               .transpose(2, 3, 1, 0, 4).reshape(2, 2, 128, 2048))
    su2 = np.ascontiguousarray(suT.reshape(8, 128, 2, 2, 256)
                               .transpose(2, 3, 1, 0, 4).reshape(2, 2, 128, 2048))
    sd2 = np.ascontiguousarray(sdT.reshape(2, 4, 128, 1024)
                               .transpose(0, 2, 1, 3).reshape(2, 128, 4096))
    wq2 = np.ascontiguousarray(wq_r.reshape(8, 128, 8, 128)
                               .transpose(2, 1, 0, 3).reshape(8, 128, 1024))
    wo2 = np.ascontiguousarray(wo_r.reshape(8, 128, 8, 128)
                               .transpose(2, 1, 0, 3).reshape(8, 128, 1024))
    wkl0 = np.asarray(inputs["L0_attn_Wk"], f32)
    wvl0 = np.asarray(inputs["L0_attn_Wv"], f32)
    wk2 = np.ascontiguousarray(wkl0.reshape(8, 128, 2, 128)
                               .transpose(2, 1, 0, 3).reshape(2, 128, 1024))
    wv2 = np.ascontiguousarray(wvl0.reshape(8, 128, 256)
                               .transpose(1, 0, 2).reshape(128, 2048))
    rwT = np.ascontiguousarray(np.asarray(inputs["L0_ffn_router_weight"], f32).T)
    rwp = np.zeros((128, 1024), bf)
    for hc in range(8):
        rwp[:, hc * 128: hc * 128 + 16] = rwT[hc * 128:(hc + 1) * 128, :].astype(bf)
    ebc = np.ascontiguousarray(np.broadcast_to(
        np.asarray(inputs["L0_ffn_e_bias"], f32)[None, :], (128, 16)))

    wqkv = np.asarray(inputs["L1_attn_W_qkv"], f32)
    bqkv = np.asarray(inputs["L1_attn_b_qkv"], f32)
    bqk = np.ascontiguousarray(bqkv[:2048].reshape(16, 128).T)
    bvbc = np.ascontiguousarray(np.broadcast_to(bqkv[2048:][None, :], (128, D)))
    wout = np.asarray(inputs["L1_attn_W_out"], f32)
    bout = np.ascontiguousarray(np.asarray(inputs["L1_attn_b_out"], f32).reshape(8, 128).T)
    wq12 = np.ascontiguousarray(wqkv[:, 0:1024].reshape(8, 128, 8, 128)
                                .transpose(2, 1, 0, 3).reshape(8, 128, 1024))
    wk12 = np.ascontiguousarray(wqkv[:, 1024:2048].reshape(8, 128, 8, 128)
                                .transpose(2, 1, 0, 3).reshape(8, 128, 1024))
    wv12 = np.ascontiguousarray(wqkv[:, 2048:3072].reshape(8, 128, 1024)
                                .transpose(1, 0, 2).reshape(128, 8192))
    wo12 = np.ascontiguousarray(wout.reshape(8, 128, 8, 128)
                                .transpose(2, 1, 0, 3).reshape(8, 128, 1024))
    g1T = np.asarray(inputs["L1_ffn_w_gate"], np.int32).T
    u1T = np.asarray(inputs["L1_ffn_w_up"], np.int32).T
    d1T = np.asarray(inputs["L1_ffn_w_down"], np.int32).T
    g12 = np.ascontiguousarray(g1T.reshape(8, 128, 24, 128)
                               .transpose(2, 1, 0, 3).reshape(24, 128, 1024))
    u12 = np.ascontiguousarray(u1T.reshape(8, 128, 24, 128)
                               .transpose(2, 1, 0, 3).reshape(24, 128, 1024))
    d12 = np.ascontiguousarray(d1T.reshape(24, 128, 8, 128)
                               .transpose(2, 1, 0, 3).reshape(8, 128, 3072))

    R2 = np.zeros((128, 128), f32)
    for s in range(2):
        for d in range(32):
            R2[s * 64 + d, s * 64 + d + 32] = -1.0
            R2[s * 64 + d + 32, s * 64 + d] = 1.0
    r2t = np.ascontiguousarray(R2.T).astype(bf)

    inv = 1.0 / (10000.0 ** (np.arange(0, HD, 2, dtype=np.float64) / HD))
    tpos = np.arange(T, dtype=np.float64)
    fr = np.outer(tpos, inv)
    emb = np.concatenate([fr, fr], axis=1)
    cos_t = np.cos(emb).astype(f32)
    sin_t = np.sin(emb).astype(f32)
    cs = np.zeros((128, 2 * T), f32)
    for s in range(2):
        cs[s * 64:(s + 1) * 64, :T] = cos_t.T
        cs[s * 64:(s + 1) * 64, T:] = sin_t.T

    cm = np.zeros((128, 4 * TC), f32)
    for j in range(4):
        kk = np.arange(128)[:, None] + 128 * j
        qq = np.arange(TC)[None, :]
        cm[:, j * TC:(j + 1) * TC] = (kk <= qq).astype(f32)

    shared = dict(
        l0_wq2=wq2, l0_wk2=wk2, l0_wv2=wv2, l0_wo2=wo2, l0_eb=ebc,
        l0_g2=g2, l0_u2=u2, l0_d2=d2, l0_sg2=sg2, l0_su2=su2, l0_sd2=sd2,
        l1_wq2=wq12, l1_wk2=wk12, l1_wv2=wv12, l1_wo2=wo12,
        l1_bqk=bqk, l1_bvbc=bvbc, l1_bout=bout,
        l1_g2=g12, l1_u2=u12, l1_d2=d12,
        ident=np.eye(128, dtype=f32),
        colones=np.ascontiguousarray(
            (np.arange(128)[None, :] == 0).astype(f32) *
            np.ones((128, 1), f32)),
        rwp_b=rwp,
        r2t=r2t, cs2=cs, cmask=cm.astype(bf),
        epsc=np.full((128, 1), EPS, f32),
        br0=np.ascontiguousarray(
            (np.arange(128)[:, None] == 0).astype(bf) *
            np.ones((1, 128), bf)),
        oneh=np.ascontiguousarray(
            (np.arange(16)[:, None] == (np.arange(16 * 128) // 128)[None, :])
            .astype(bf)),
    )
    in_maps = []
    for c in range(NCORE):
        m = dict(shared)
        m["x_in"] = np.ascontiguousarray(x[c].T)
        in_maps.append(m)
    return in_maps


def kernel(**inputs):
    in_maps = _host_inputs(inputs)
    nc, _ = build_program(debug=False)
    res = run_bass_kernel_spmd(nc, in_maps, core_ids=list(range(NCORE)))
    outs = [np.ascontiguousarray(np.asarray(r["out"]).T) for r in res.results]
    return np.stack(outs, axis=0).astype(np.float32)


# version 19
# speedup vs baseline: 1.0123x; 1.0123x over previous
import numpy as np
import ml_dtypes

import concourse.bass as bass
import concourse.mybir as mybir
from concourse.bass_utils import run_bass_kernel_spmd

F32 = mybir.dt.float32
BF16 = mybir.dt.bfloat16
I32 = mybir.dt.int32
AF = mybir.ActivationFunctionType
ALU = mybir.AluOpType
AX = mybir.AxisListType

D = 1024
T = 2048
NCORE = 8
HD = 64
TC = 512
NTC = T // TC
NHC = D // 128
EPS = 1e-5
NE = 16
NXP = NE + 2
FF1 = 3072
NIC1 = FF1 // 128
DQ = 1.0 / 127.0

L0_PAIRS = [(0, 4), (1, 5), (2, 6), (3, 7), (8, 12), (9, 13), (10, 14), (11, 15)]


class Sem:
    def __init__(self, nc, name):
        self.h = nc.alloc_semaphore(name=name)
        self.v = 0


class Op:
    __slots__ = ("eng", "fn", "deps", "sem", "val", "name")

    def __init__(self, eng, fn, deps, sem, name):
        self.eng, self.fn, self.deps, self.sem, self.name = eng, fn, deps, sem, name
        self.val = None


class Plan:

    def __init__(self, nc):
        self.nc = nc
        self.ops = []
        self.esem = {e: Sem(nc, f"sem_{e}") for e in ("T", "S", "V")}

    def add(self, eng, fn, deps=(), sem=None, name=""):
        sem = sem if sem is not None else self.esem.get(eng)
        assert sem is not None
        op = Op(eng, fn, [d for d in deps if d is not None], sem, name)
        self.ops.append(op)
        return op

    def emit(self, block):
        for op in self.ops:
            if op.eng in ("G", "Q"):
                n = op.fn(None)
                op.sem.v += 16 * n
            else:
                op.sem.v += 1
            op.val = op.sem.v

        streams = {e: [op for op in self.ops if op.eng == e]
                   for e in ("T", "S", "V", "G", "Q")}

        def make(eng_key):
            ops = streams[eng_key]

            def section(eng):
                waited = {}
                for op in ops:
                    if op.eng in ("V", "S") and op.val > 1:
                        k = id(op.sem)
                        if waited.get(k, -1) < op.val - 1:
                            eng.wait_ge(op.sem.h, op.val - 1)
                            waited[k] = op.val - 1
                    want = {}
                    for d in op.deps:
                        k = id(d.sem)
                        if k not in want or want[k][1] < d.val:
                            want[k] = (d.sem, d.val)
                    for k, (sm, v) in want.items():
                        if waited.get(k, -1) < v:
                            eng.wait_ge(sm.h, v)
                            waited[k] = v
                    last = op.fn(eng)
                    if op.eng in ("G", "Q"):
                        for ins in (last if isinstance(last, (list, tuple)) else
                                    ([last] if last is not None else [])):
                            ins.then_inc(op.sem.h, 16)
                    else:
                        if last is None:
                            last = eng.drain()
                        last.then_inc(op.sem.h, 1)
            return section

        block.gpsimd(make("G"))
        block.sync(make("Q"))
        block.tensor(make("T"))
        block.scalar(make("S"))
        block.vector(make("V"))


class Region:
    def __init__(self, tensor, ncols, name):
        self.t, self.ncols, self.name = tensor, ncols, name
        self.off = 0
        self.marks = []
        self.peak = 0

    def alloc(self, n):
        assert self.off + n <= self.ncols, \
            f"{self.name} overflow: {self.off}+{n} > {self.ncols}"
        o = self.off
        self.off += n
        self.peak = max(self.peak, self.off)
        return o

    def push(self):
        self.marks.append(self.off)

    def pop(self):
        self.off = self.marks.pop()


def build_program(debug=False):
    nc = bass.Bass("TRN2", target_bir_lowering=False)
    dt = nc.dram_tensor

    x_in = dt("x_in", [D, T], F32, kind="ExternalInput").ap()
    out_t = dt("out", [D, T], F32, kind="ExternalOutput").ap()
    x_res = dt("x_res", [D, T], F32, kind="Internal").ap()
    s_g = dt("s_g", [NXP, 2, 128, 2048], BF16, kind="Internal").ap()
    s_u = dt("s_u", [NXP, 2, 128, 2048], BF16, kind="Internal").ap()
    s_d = dt("s_d", [NXP, 128, 4096], BF16, kind="Internal").ap()

    l0_wq2 = dt("l0_wq2", [8, 128, 1024], F32, kind="ExternalInput").ap()
    l0_wk2 = dt("l0_wk2", [2, 128, 1024], F32, kind="ExternalInput").ap()
    l0_wv2 = dt("l0_wv2", [128, 2048], F32, kind="ExternalInput").ap()
    l0_wo2 = dt("l0_wo2", [8, 128, 1024], F32, kind="ExternalInput").ap()
    l0_eb = dt("l0_eb", [128, 16], F32, kind="ExternalInput").ap()
    l0_g2 = dt("l0_g2", [NE, 2, 128, 2048], F32, kind="ExternalInput").ap()
    l0_u2 = dt("l0_u2", [NE, 2, 128, 2048], F32, kind="ExternalInput").ap()
    l0_d2 = dt("l0_d2", [NE, 128, 4096], F32, kind="ExternalInput").ap()
    l0_sg2 = dt("l0_sg2", [2, 2, 128, 2048], F32, kind="ExternalInput").ap()
    l0_su2 = dt("l0_su2", [2, 2, 128, 2048], F32, kind="ExternalInput").ap()
    l0_sd2 = dt("l0_sd2", [2, 128, 4096], F32, kind="ExternalInput").ap()

    l1_wq2 = dt("l1_wq2", [8, 128, 1024], F32, kind="ExternalInput").ap()
    l1_wk2 = dt("l1_wk2", [8, 128, 1024], F32, kind="ExternalInput").ap()
    l1_wv2 = dt("l1_wv2", [128, 8192], F32, kind="ExternalInput").ap()
    l1_bqk = dt("l1_bqk", [128, 16], F32, kind="ExternalInput").ap()
    l1_bvbc = dt("l1_bvbc", [128, D], F32, kind="ExternalInput").ap()
    l1_wo2 = dt("l1_wo2", [8, 128, 1024], F32, kind="ExternalInput").ap()
    l1_bout = dt("l1_bout", [128, NHC], F32, kind="ExternalInput").ap()
    l1_g2 = dt("l1_g2", [NIC1, 128, 1024], I32, kind="ExternalInput").ap()
    l1_u2 = dt("l1_u2", [NIC1, 128, 1024], I32, kind="ExternalInput").ap()
    l1_d2 = dt("l1_d2", [NHC, 128, 3072], I32, kind="ExternalInput").ap()

    ident = dt("ident", [128, 128], F32, kind="ExternalInput").ap()
    colones = dt("colones", [128, 128], F32, kind="ExternalInput").ap()
    rwp_b = dt("rwp_b", [128, 1024], BF16, kind="ExternalInput").ap()
    r2t = dt("r2t", [128, 128], BF16, kind="ExternalInput").ap()
    cs2 = dt("cs2", [128, 2 * T], F32, kind="ExternalInput").ap()
    cmask = dt("cmask", [128, 4 * TC], BF16, kind="ExternalInput").ap()
    oneh = dt("oneh", [16, 16 * 128], BF16, kind="ExternalInput").ap()
    epsc = dt("epsc", [128, 1], F32, kind="ExternalInput").ap()
    br0 = dt("br0", [128, 128], BF16, kind="ExternalInput").ap()

    dbg = {}

    def dbg_out(name, shape):
        if debug:
            dbg[name] = dt(name, shape, F32, kind="ExternalOutput").ap()
            return dbg[name]
        return None

    d_xn = [dbg_out("d_xn0", [D, T]), dbg_out("d_xn1", [D, T])]
    d_sc = dbg_out("d_sc0", [128, 64 * NTC])
    d_C = dbg_out("d_C0", [16, T])
    d_g = [dbg_out("d_g0", [D, T]), dbg_out("d_g1", [D, T])]
    d_oall = [dbg_out("d_oall0", [128, NHC * TC]), dbg_out("d_oall1", [128, NHC * TC])]
    d_sums = [dbg_out("d_sums0", [32, TC]), dbg_out("d_sums1", [32, TC])]

    A16 = 55296
    A32 = 5120
    with (
        nc.sbuf_tensor("xn_sb", [128, NHC * T], BF16) as xn_sb,
        nc.sbuf_tensor("g_sb", [128, NHC * T], BF16) as g_sb,
        nc.sbuf_tensor("a16t", [128, A16], BF16) as a16t,
        nc.sbuf_tensor("a32t", [128, A32], F32) as a32t,
        nc.sbuf_tensor("cst32", [128, 192], F32) as cst32,
        nc.sbuf_tensor("cst16", [128, 1280], BF16) as cst16,
        nc.sbuf_tensor("axr", [128, 2048], mybir.dt.float32r) as axr,
        nc.sbuf_tensor("cstr", [128, 128], mybir.dt.float32r) as cstr,
        nc.sbuf_tensor("oneh_sb", [16, 16 * 128], BF16) as oneh_sb,
        nc.psum_tensor("pst0", [128, 1024], F32) as pst0,
        nc.psum_tensor("pst1", [128, 1024], F32) as pst1,
        nc.psum_tensor("pb4", [128, 512], F32) as pb4,
        nc.psum_tensor("pb5", [128, 512], F32) as pb5,
        nc.psum_tensor("pb6", [128, 512], F32) as pb6,
        nc.psum_tensor("pb7", [128, 512], F32) as pb7,
        nc.Block() as block,
    ):
        pst = [pst0, pst1]
        pb0, pb1 = pst0[:, 0:512], pst0[:, 512:1024]
        pb2, pb3 = pst1[:, 0:512], pst1[:, 512:1024]
        pb = [pb0, pb1, pb2, pb3, pb4, pb5, pb6, pb7]
        P = Plan(nc)
        a16 = Region(a16t, A16, "a16")
        a32 = Region(a32t, A32, "a32")

        sW = [Sem(nc, f"sW{i}") for i in range(4)]
        sWu = [Sem(nc, f"sWu{i}") for i in range(2)]
        sWd = [Sem(nc, f"sWd{i}") for i in range(2)]
        sWB = [Sem(nc, f"sWB{i}") for i in range(4)]
        sXa = [Sem(nc, f"sXa{i}") for i in range(3)]
        sXb = [Sem(nc, f"sXb{i}") for i in range(2)]
        sXl = [Sem(nc, f"sXl{i}") for i in range(2)]
        sO = Sem(nc, "sO")
        sO2 = Sem(nc, "sO2")
        sCst = Sem(nc, "sCst")
        sDbg = Sem(nc, "sDbg")
        dbg_prev = [None]
        sDbq = Sem(nc, "sDbq")
        dbq_prev = [None]
        sCS = Sem(nc, "sCS")
        sKW = [Sem(nc, "sKW0"), Sem(nc, "sKW1")]
        sVW = Sem(nc, "sVW")
        sWO = [Sem(nc, "sWO0"), Sem(nc, "sWO1")]
        sQW = [Sem(nc, "sQW0"), Sem(nc, "sQW1")]

        C_ID, C_EB, C_BQK, C_BOUT, C_EPS = 0, 128, 144, 160, 168
        B_R2T, B_BR0, B_RWP = 0, 128, 256

        def ld_consts(g):
            if g is None:
                return 10
            return [
                g.dma_start(out=cst32[:, C_ID:C_ID + 128], in_=ident[:, :]),
                g.dma_start(out=cst32[:, C_EB:C_EB + 16], in_=l0_eb[:, :]),
                g.dma_start(out=cst32[:, C_BQK:C_BQK + 16], in_=l1_bqk[:, :]),
                g.dma_start(out=cst32[:, C_BOUT:C_BOUT + 8], in_=l1_bout[:, :]),
                g.dma_start(out=cst16[:, B_R2T:B_R2T + 128], in_=r2t[:, :]),
                g.dma_start(out=oneh_sb[:, :], in_=oneh[:, :]),
                g.dma_start(out=cst32[:, C_EPS:C_EPS + 1], in_=epsc[:, :]),
                g.dma_start(out=cst16[:, B_BR0:B_BR0 + 128], in_=br0[:, :]),
                g.dma_start(out=cst16[:, B_RWP:B_RWP + 1024], in_=rwp_b[:, :]),
                g.dma_start(out=cstr[:, :], in_=colones[:, :]),
            ]
        op_cst = P.add("G", ld_consts, sem=sCst, name="consts")

        fence0 = [op_cst]

        def phase_A(L, fence):
            xsrc = x_in if L == 0 else x_res
            o_bcs = A16 - 1024
            a32.push()
            o_x2 = [a32.alloc(TC) for _ in range(2)]
            o_mean = a32.alloc(TC)
            o_var = a32.alloc(TC)
            o_rstd = a32.alloc(TC)
            o_nmr = a32.alloc(TC)
            o_t1 = a32.alloc(TC)
            o_xt2 = a32.alloc(TC)
            if L == 0:
                o_srT = a32.alloc(512)
                o_sraw = a32.alloc(64)
                o_z = a32.alloc(64)
                o_sc = a32.alloc(64)
                o_scb = a32.alloc(64)
                o_rs = a32.alloc(160)
                o_Ct = a32.alloc(64)
            AXA = [0, TC, 2 * TC]
            AXQ = 3 * TC

            op_msA = P.add("V", lambda v: v.memset(
                a16t[:, o_bcs:o_bcs + 1024], 0.0), deps=fence, name=f"A{L}ms")

            prev = {}
            ct_ops = []
            for tc in range(NTC):
                for hc in range(NHC):
                    gi = tc * NHC + hc
                    xp = gi % 3

                    def f_ld(g, tc=tc, hc=hc, xp=xp):
                        if g is None:
                            return 1
                        return g.dma_start(
                            out=axr[:, AXA[xp]:AXA[xp] + TC],
                            in_=xsrc[hc * 128:(hc + 1) * 128, tc * TC:(tc + 1) * TC])
                    dl = [prev.get(("p1g", gi - 3))]
                    if gi < 3:
                        dl = dl + fence
                    op_ld = P.add("G", f_ld, deps=dl, sem=sXa[xp],
                                  name=f"A{L}ld{tc}.{hc}")

                    def f_sq(s, xp=xp):
                        return s.activation(
                            axr[:, AXQ:AXQ + TC],
                            axr[:, AXA[xp]:AXA[xp] + TC].bitcast(F32), AF.Square)
                    op_sq = P.add("S", f_sq,
                                  deps=[op_ld, prev.get(("mm", tc, hc - 1))],
                                  name=f"A{L}sq{tc}.{hc}")

                    def f_mm1(t, hc=hc, xp=xp):
                        t.matmul(pb4[:, :], cstr[:, 0:128],
                                 axr[:, AXA[xp]:AXA[xp] + TC],
                                 start=(hc == 0), stop=(hc == NHC - 1))
                        return t.matmul(pb5[:, :], cstr[:, 0:128],
                                        axr[:, AXQ:AXQ + TC],
                                        start=(hc == 0), stop=(hc == NHC - 1))
                    dm = [op_ld, op_sq, prev.get(("srow", tc - 1))]
                    op_mm = P.add("T", f_mm1, deps=dm, name=f"A{L}mm{tc}.{hc}")
                    prev[("p1g", gi)] = op_mm
                    prev[("mm", tc, hc)] = op_mm

                def f_srow(v):
                    v.tensor_scalar(a32t[0:1, o_mean:o_mean + TC], pb4[0:1, :],
                                    1.0 / D, None, ALU.mult)
                    v.drain()
                    v.tensor_scalar(a32t[0:1, o_var:o_var + TC], pb5[0:1, :],
                                    1.0 / D, None, ALU.mult)
                    v.drain()
                    v.tensor_tensor(a32t[0:1, o_t1:o_t1 + TC],
                                    a32t[0:1, o_mean:o_mean + TC],
                                    a32t[0:1, o_mean:o_mean + TC], ALU.mult)
                    v.drain()
                    return v.tensor_tensor(a32t[0:1, o_var:o_var + TC],
                                           a32t[0:1, o_var:o_var + TC],
                                           a32t[0:1, o_t1:o_t1 + TC], ALU.subtract)
                op_srow = P.add("V", f_srow, deps=[prev[("mm", tc, NHC - 1)]],
                                name=f"A{L}srow{tc}")
                prev[("srow", tc)] = op_srow

                def f_sqrt(s):
                    s.activation(a32t[0:1, o_t1:o_t1 + TC],
                                 a32t[0:1, o_var:o_var + TC],
                                 AF.Ln, bias=cst32[0:1, C_EPS:C_EPS + 1])
                    s.drain()
                    return s.activation(a32t[0:1, o_rstd:o_rstd + TC],
                                        a32t[0:1, o_t1:o_t1 + TC],
                                        AF.Exp, scale=-0.5)
                op_sqrt = P.add("S", f_sqrt, deps=[op_srow],
                                name=f"A{L}rsq{tc}")

                def f_rstd(v):
                    return v.scalar_tensor_tensor(
                        a32t[0:1, o_nmr:o_nmr + TC],
                        a32t[0:1, o_mean:o_mean + TC], -1.0,
                        a32t[0:1, o_rstd:o_rstd + TC], ALU.mult, ALU.mult)
                op_rstd = P.add("V", f_rstd, deps=[op_sqrt], name=f"A{L}rstd{tc}")

                def f_brow(v):
                    with nc.allow_low_precision(reason="LN scale rows -> bf16 bcast"):
                        v.tensor_copy(a16t[0:1, o_bcs:o_bcs + TC],
                                      a32t[0:1, o_rstd:o_rstd + TC])
                        v.drain()
                        return v.tensor_copy(a16t[0:1, o_bcs + TC:o_bcs + 2 * TC],
                                             a32t[0:1, o_nmr:o_nmr + TC])
                op_brow = P.add("V", f_brow,
                                deps=[op_rstd, op_msA, prev.get(("bcmm", tc - 1))],
                                name=f"A{L}brow{tc}")

                def f_bc(t):
                    t.matmul(pb6[:, :], cst16[:, B_BR0:B_BR0 + 128],
                             a16t[:, o_bcs:o_bcs + TC], start=True, stop=True)
                    return t.matmul(pb7[:, :], cst16[:, B_BR0:B_BR0 + 128],
                                    a16t[:, o_bcs + TC:o_bcs + 2 * TC],
                                    start=True, stop=True)
                op_bc = P.add("T", f_bc, deps=[op_brow, prev.get(("xnl", tc - 1))],
                              name=f"A{L}bc{tc}")
                prev[("bcmm", tc)] = op_bc

                xn_ops = []
                for hc in range(NHC):
                    xp = hc % 2

                    def f_ld2(q, tc=tc, hc=hc, xp=xp):
                        if q is None:
                            return 1
                        return q.dma_start(
                            out=a32t[:, o_x2[xp]:o_x2[xp] + TC],
                            in_=xsrc[hc * 128:(hc + 1) * 128, tc * TC:(tc + 1) * TC])
                    dl2 = [prev.get(("xn", tc, hc - 2)),
                           prev.get(("xn", tc - 1, NHC - 2 + hc)) if hc < 2 else None]
                    if tc == 0 and hc < 2:
                        dl2 = dl2 + fence
                    op_ld2 = P.add("Q", f_ld2, deps=dl2, sem=sXb[xp],
                                   name=f"A{L}ld2{tc}.{hc}")

                    def f_xn(v, hc=hc, tc=tc, xp=xp):
                        v.tensor_tensor(a32t[:, o_xt2:o_xt2 + TC],
                                        a32t[:, o_x2[xp]:o_x2[xp] + TC],
                                        pb6[:, :], ALU.mult)
                        v.drain()
                        return v.tensor_tensor(
                            xn_sb[:, hc * T + tc * TC: hc * T + tc * TC + TC],
                            a32t[:, o_xt2:o_xt2 + TC], pb7[:, :], ALU.add)
                    op_xn = P.add("V", f_xn, deps=[op_bc, op_ld2],
                                  name=f"A{L}xn{tc}.{hc}")
                    prev[("xn", tc, hc)] = op_xn
                    xn_ops.append(op_xn)
                prev[("xnl", tc)] = xn_ops[-1]

                if L == 0:
                    def f_rsc(t, tc=tc):
                        for hc in range(NHC):
                            li = t.matmul(
                                pst0[:, 0:512],
                                cst16[:, B_RWP + hc * 128: B_RWP + (hc + 1) * 128],
                                xn_sb[:, hc * T + tc * TC: hc * T + tc * TC + TC],
                                start=(hc == 0), stop=(hc == NHC - 1))
                        return li
                    op_rsc = P.add("T", f_rsc,
                                   deps=[prev[("xnl", tc)],
                                         prev.get(("rscE", tc - 1))],
                                   name=f"A0rsc{tc}")

                    def f_rawT(v):
                        return v.tensor_copy(a32t[0:16, o_srT:o_srT + TC],
                                             pst0[0:16, 0:512])
                    op_rawT = P.add("V", f_rawT, deps=[op_rsc], name=f"A0rawT{tc}")
                    prev[("rscE", tc)] = op_rawT

                    def f_rtr(t):
                        for tt in range(4):
                            li = t.transpose(
                                pst0[:, 512 + tt * 16:512 + (tt + 1) * 16],
                                a32t[0:16, o_srT + tt * 128:
                                     o_srT + (tt + 1) * 128],
                                cst32[0:16, C_ID:C_ID + 16])
                        return li
                    op_rtr = P.add("T", f_rtr,
                                   deps=[op_rawT, prev.get(("rtrE", tc - 1))],
                                   name=f"A0rtr{tc}")

                    def f_rawe(v):
                        return v.tensor_copy(a32t[:, o_sraw:o_sraw + 64],
                                             pst0[:, 512:576])
                    op_rawe = P.add("V", f_rawe, deps=[op_rtr], name=f"A0rawe{tc}")
                    prev[("rtrE", tc)] = op_rawe

                    sig_ops = []
                    for tt in range(4):
                        def f_sig(s, tt=tt):
                            return s.activation(
                                a32t[:, o_sc + tt * 16: o_sc + (tt + 1) * 16],
                                a32t[:, o_sraw + tt * 16: o_sraw + (tt + 1) * 16],
                                AF.Sigmoid)
                        sig_ops.append(P.add("S", f_sig, deps=[op_rawe],
                                             name=f"A0sig{tc}.{tt}"))

                    def f_scb(v):
                        for tt in range(4):
                            li = v.tensor_tensor(
                                a32t[:, o_scb + tt * 16: o_scb + (tt + 1) * 16],
                                a32t[:, o_sc + tt * 16: o_sc + (tt + 1) * 16],
                                cst32[:, C_EB:C_EB + 16], ALU.add)
                        return li
                    op_scb = P.add("V", f_scb, deps=sig_ops, name=f"A0scb{tc}")

                    rA, rB, rGS, rMID, rLH, rM2G, rGM, rCUR, rCI, rSB = \
                        0, 32, 64, 80, 96, 112, 116, 0, 32, 34
                    S_ = a32t[:, o_scb:o_scb + 64]

                    def f_rt1(v):
                        v.tensor_tensor(a32t[:, o_rs + rA:o_rs + rA + 32],
                                        S_[:, 0:64:2], S_[:, 1:64:2], ALU.max)
                        v.drain()
                        v.tensor_tensor(a32t[:, o_rs + rB:o_rs + rB + 32],
                                        S_[:, 0:64:2], S_[:, 1:64:2], ALU.min)
                        v.drain()
                        A2 = a32t[:, o_rs + rA:o_rs + rA + 32]
                        B2 = a32t[:, o_rs + rB:o_rs + rB + 32]
                        v.tensor_tensor(a32t[:, o_rs + rGS:o_rs + rGS + 16],
                                        A2[:, 0:32:2], A2[:, 1:32:2], ALU.max)
                        v.drain()
                        v.tensor_tensor(a32t[:, o_rs + rMID:o_rs + rMID + 16],
                                        A2[:, 0:32:2], A2[:, 1:32:2], ALU.min)
                        v.drain()
                        v.tensor_tensor(a32t[:, o_rs + rLH:o_rs + rLH + 16],
                                        B2[:, 0:32:2], B2[:, 1:32:2], ALU.max)
                        v.drain()
                        v.tensor_tensor(a32t[:, o_rs + rMID:o_rs + rMID + 16],
                                        a32t[:, o_rs + rMID:o_rs + rMID + 16],
                                        a32t[:, o_rs + rLH:o_rs + rLH + 16], ALU.max)
                        v.drain()
                        return v.tensor_tensor(a32t[:, o_rs + rGS:o_rs + rGS + 16],
                                               a32t[:, o_rs + rGS:o_rs + rGS + 16],
                                               a32t[:, o_rs + rMID:o_rs + rMID + 16],
                                               ALU.add)
                    op_rt1 = P.add("V", f_rt1, deps=[op_scb], name=f"A0rt1{tc}")

                    def f_rt2(v):
                        GS = a32t[:, o_rs + rGS:o_rs + rGS + 16]
                        v.tensor_tensor(a32t[:, o_rs + rMID:o_rs + rMID + 8],
                                        GS[:, 0:16:2], GS[:, 1:16:2], ALU.max)
                        v.drain()
                        v.tensor_tensor(a32t[:, o_rs + rLH:o_rs + rLH + 8],
                                        GS[:, 0:16:2], GS[:, 1:16:2], ALU.min)
                        v.drain()
                        PH = a32t[:, o_rs + rMID:o_rs + rMID + 8]
                        PL = a32t[:, o_rs + rLH:o_rs + rLH + 8]
                        v.tensor_tensor(a32t[:, o_rs + rM2G:o_rs + rM2G + 4],
                                        PH[:, 0:8:2], PH[:, 1:8:2], ALU.min)
                        v.drain()
                        v.tensor_tensor(a32t[:, o_rs + rLH + 8:o_rs + rLH + 12],
                                        PL[:, 0:8:2], PL[:, 1:8:2], ALU.max)
                        v.drain()
                        v.tensor_tensor(a32t[:, o_rs + rM2G:o_rs + rM2G + 4],
                                        a32t[:, o_rs + rM2G:o_rs + rM2G + 4],
                                        a32t[:, o_rs + rLH + 8:o_rs + rLH + 12],
                                        ALU.max)
                        v.drain()
                        for gg in range(4):
                            v.tensor_tensor(
                                a32t[:, o_rs + rGM + gg:o_rs + rGM + gg + 13:4],
                                GS[:, gg:16:4],
                                a32t[:, o_rs + rM2G:o_rs + rM2G + 4], ALU.is_ge)
                        v.drain()
                        for j in range(4):
                            li = v.tensor_tensor(
                                a32t[:, o_z + j:o_z + j + 61:4],
                                S_[:, j:64:4],
                                a32t[:, o_rs + rGM:o_rs + rGM + 16], ALU.mult)
                        return li
                    op_rt2 = P.add("V", f_rt2, deps=[op_rt1], name=f"A0rt2{tc}")

                    def f_top4(v):
                        cur = a32t[:, o_rs + 0:o_rs + 64]
                        cur3 = cur.rearrange("p (g e) -> p g e", e=16)
                        selb = a32t[:, o_rs + 64:o_rs + 128]
                        selb3 = selb.rearrange("p (g e) -> p g e", e=16)
                        ci = a32t[:, o_rs + 128:o_rs + 132]
                        ci3 = ci.rearrange("p (g e) -> p g e", e=1)
                        cib = ci3.broadcast_to((128, 4, 16))
                        ci2 = a32t[:, o_rs + 132:o_rs + 136]
                        ci2b = ci2.rearrange("p (g e) -> p g e", e=1)\
                            .broadcast_to((128, 4, 16))
                        mk = a32t[:, o_z:o_z + 64]
                        mk3 = mk.rearrange("p (g e) -> p g e", e=16)
                        sc3 = a32t[:, o_sc:o_sc + 64]
                        Ct3 = a32t[:, o_Ct:o_Ct + 64].rearrange(
                            "p (g e) -> p g e", e=16)
                        v.tensor_copy(cur, mk)
                        for _ in range(3):
                            v.drain()
                            v.reduce_max(ci3, cur3, axis=AX.X)
                            v.drain()
                            v.tensor_tensor(selb3, cur3, cib, ALU.is_ge)
                            v.drain()
                            v.scalar_tensor_tensor(cur, selb, -1e9, cur,
                                                   ALU.mult, ALU.add)
                        v.drain()
                        v.reduce_max(ci3, cur3, axis=AX.X)
                        v.drain()
                        v.tensor_tensor(selb3, mk3, cib, ALU.is_ge)
                        v.drain()
                        v.tensor_tensor(cur, selb, sc3, ALU.mult)
                        v.drain()
                        v.reduce_sum(ci3, cur3, axis=AX.X)
                        v.drain()
                        v.tensor_scalar(ci, ci, 1e-20, None, ALU.add)
                        v.drain()
                        v.reciprocal(ci2, ci)
                        v.drain()
                        return v.scalar_tensor_tensor(Ct3, cur3, 2.5, ci2b,
                                                      ALU.mult, ALU.mult)
                    op_t4 = P.add("V", f_top4, deps=[op_rt2], name=f"A0t4{tc}")

                    for tt in range(4):

                        def f_ctr(t, tt=tt):
                            dstb = pst1[:, 0:128] if tt % 2 == 0 else pst1[:, 512:640]
                            return t.transpose(
                                dstb[0:16, 0:128],
                                a32t[:, o_Ct + tt * 16: o_Ct + (tt + 1) * 16],
                                cst32[:, C_ID:C_ID + 128])
                        dtr = [op_t4]
                        if tt >= 2:
                            dtr.append(ct_ops[-2][0])
                        else:
                            dtr.append(prev.get(("ctE", tc - 1)))
                        op_ctr = P.add("T", f_ctr, deps=dtr, name=f"A0ctr{tc}.{tt}")

                        def f_cte(v, tt=tt, tc=tc):
                            srcb = pst1[:, 0:128] if tt % 2 == 0 else pst1[:, 512:640]
                            return v.tensor_copy(
                                cfm_sb[0:16, tc * TC + tt * 128:
                                       tc * TC + (tt + 1) * 128],
                                srcb[0:16, 0:128])
                        op_cte = P.add("V", f_cte, deps=[op_ctr],
                                       name=f"A0cte{tc}.{tt}")
                        ct_ops.append((op_cte, op_ctr))
                    prev[("ctE", tc)] = ct_ops[-1][0]

            res = dict(xn_last=prev[("xnl", NTC - 1)])
            res["xn_tc"] = [prev[("xnl", t_)] for t_ in range(NTC)]
            res["lasts"] = [res["xn_last"]]
            if L == 0:
                res["c_last"] = ct_ops[-1][0]
                res["c_tc"] = [ct_ops[4 * t_ + 3][0] for t_ in range(NTC)]
                res["lasts"].append(res["c_last"])
            else:
                res["c_last"] = None
                res["lasts"].append(prev[("srow", NTC - 1)])
            a32.pop()
            return res

        def phase_B0(A, ldfence):
            wb_ring = [None, None, None, None]
            wb_i = [0]
            wb_map = {}
            a16.push()
            o_gb = [a16.alloc(2048) for _ in range(2)]
            o_ub = [a16.alloc(2048) for _ in range(2)]
            o_db = [a16.alloc(4096) for _ in range(2)]
            o_hp = a16.alloc(NXP * 2048)
            a32.push()
            o_silu = [a32.alloc(TC) for _ in range(2)]
            o_stg = [a32.alloc(2048) for _ in range(2)]

            prev = {}
            hidx = 0
            gevac_prev = None
            dn_last_prev = None
            for tc in range(NTC):
                hlast = []
                cbc0 = None

                def f_cbc0(t, tc=tc):
                    return t.matmul(pb4[:, :], oneh_sb[0:16, 0:128],
                                    cfm_sb[0:16, tc * TC:(tc + 1) * TC],
                                    start=True, stop=True)
                dc0 = [A["c_last"], prev.get(("pcb", 0)), gevac_prev]
                if tc == 0:
                    dc0 += A["lasts"]
                cbc0 = P.add("T", f_cbc0, deps=dc0, name=f"B0cbc{tc}.0")
                prev[("cbc", 0)] = cbc0

                for e in range(NXP):
                    hp_e = o_hp + e * 2048
                    e_h1 = []
                    for half in range(2):
                        wp = hidx % 2
                        if tc == 0:
                            if e < NE:
                                gsrc = l0_g2[e, half]
                                usrc = l0_u2[e, half]
                            else:
                                gsrc = l0_sg2[e - NE, half]
                                usrc = l0_su2[e - NE, half]
                        else:
                            gsrc = s_g[e, half]
                            usrc = s_u[e, half]

                        def f_ldg(q, gsrc=gsrc, wp=wp):
                            if q is None:
                                return 1
                            return q.dma_start(
                                out=a16t[:, o_gb[wp]:o_gb[wp] + 2048], in_=gsrc)

                        def f_ldu(q, usrc=usrc, wp=wp):
                            if q is None:
                                return 1
                            return q.dma_start(
                                out=a16t[:, o_ub[wp]:o_ub[wp] + 2048], in_=usrc)
                        bp = prev.get(("gubuf", hidx - 2))
                        fen = ldfence if hidx < 2 else []
                        wbg_b = prev.get(("wbgb", wp))
                        wbu_b = prev.get(("wbub", wp))
                        if tc == 0:
                            opldg = P.add("G", f_ldg, deps=[bp, wbg_b] + fen,
                                          sem=sW[wp], name=f"B0ldg{tc}.{e}.{half}")

                            def f_ldur(q, usrc=usrc, wp=wp):
                                if q is None:
                                    return 1
                                return q.dma_start(
                                    out=a32t[:, o_stg[wp]:o_stg[wp] + 2048],
                                    in_=usrc)
                            opldur = P.add("Q", f_ldur,
                                           deps=[prev.get(("stg", wp))] + fen,
                                           sem=sWu[wp],
                                           name=f"B0ldur{tc}.{e}.{half}")

                            def f_ucast(v, wp=wp):
                                with nc.allow_low_precision(
                                        reason="weight stream bf16 cast"):
                                    return v.tensor_copy(
                                        a16t[:, o_ub[wp]:o_ub[wp] + 2048],
                                        a32t[:, o_stg[wp]:o_stg[wp] + 2048])
                            opldu = P.add("V", f_ucast,
                                          deps=[opldur, bp, wbu_b],
                                          name=f"B0ucast{e}.{half}")
                            prev[("stg", wp)] = opldu

                            def f_wbg(q, e=e, half=half, wp=wp):
                                if q is None:
                                    return 1
                                return q.dma_start(
                                    out=s_g[e, half],
                                    in_=a16t[:, o_gb[wp]:o_gb[wp] + 2048])

                            def f_wbu(q, e=e, half=half, wp=wp):
                                if q is None:
                                    return 1
                                return q.dma_start(
                                    out=s_u[e, half],
                                    in_=a16t[:, o_ub[wp]:o_ub[wp] + 2048])
                            wi = wb_i[0]
                            op_wbg = P.add("Q", f_wbg,
                                           deps=[opldg, opldu, wb_ring[wi % 4]],
                                           sem=sWB[wi % 4], name=f"B0wbg{e}.{half}")
                            wb_ring[wi % 4] = op_wbg
                            wb_i[0] += 1
                            wi = wb_i[0]
                            op_wbu = P.add("Q", f_wbu,
                                           deps=[opldu, wb_ring[wi % 4]],
                                           sem=sWB[wi % 4], name=f"B0wbu{e}.{half}")
                            wb_ring[wi % 4] = op_wbu
                            wb_i[0] += 1
                            wb_map[("g", e, half)] = op_wbg
                            wb_map[("u", e, half)] = op_wbu
                            prev[("wbgb", wp)] = op_wbg
                            prev[("wbub", wp)] = op_wbu
                        else:
                            dg = [bp, wbg_b, wb_map.get(("g", e, half))] + fen
                            du = [bp, wbu_b, wb_map.get(("u", e, half))] + fen
                            opldg = P.add("G", f_ldg, deps=dg, sem=sW[wp],
                                          name=f"B0ldg{tc}.{e}.{half}")
                            opldu = P.add("Q", f_ldu, deps=du, sem=sWu[wp],
                                          name=f"B0ldu{tc}.{e}.{half}")

                        for ici in range(2):
                            ic = half * 2 + ici
                            gp_ = ic % 2
                            up_ = 2 + ic % 2

                            def f_gmm(t, wp=wp, ici=ici, gp_=gp_, tc=tc):
                                for hc in range(NHC):
                                    li = t.matmul(
                                        pb[gp_][:, :],
                                        a16t[:, o_gb[wp] + hc * 256 + ici * 128:
                                             o_gb[wp] + hc * 256 + ici * 128 + 128],
                                        xn_sb[:, hc * T + tc * TC: hc * T + tc * TC + TC],
                                        start=(hc == 0), stop=(hc == NHC - 1))
                                return li
                            dg = [opldg, opldu, prev.get(("pg", gp_)), A["xn_last"]]
                            if tc > 0 and e == 0 and ic < 2:
                                dg.append(gevac_prev)
                            op_gmm = P.add("T", f_gmm, deps=dg, name=f"B0g{tc}.{e}.{ic}")

                            def f_umm(t, wp=wp, ici=ici, up_=up_, tc=tc):
                                for hc in range(NHC):
                                    li = t.matmul(
                                        pb[up_][:, :],
                                        a16t[:, o_ub[wp] + hc * 256 + ici * 128:
                                             o_ub[wp] + hc * 256 + ici * 128 + 128],
                                        xn_sb[:, hc * T + tc * TC: hc * T + tc * TC + TC],
                                        start=(hc == 0), stop=(hc == NHC - 1))
                                return li
                            du = [opldu, opldg, prev.get(("pu", up_)), A["xn_last"]]
                            if tc > 0 and e == 0 and ic < 2:
                                du.append(gevac_prev)
                            op_umm = P.add("T", f_umm, deps=du, name=f"B0u{tc}.{e}.{ic}")

                            def f_silu(s, ic=ic, gp_=gp_):
                                return s.activation(
                                    a32t[:, o_silu[ic % 2]:o_silu[ic % 2] + TC],
                                    pb[gp_][:, :], AF.Silu)
                            dsi = [op_gmm, prev.get(("h1p", ic % 2))]
                            op_si = P.add("S", f_silu, deps=dsi, name=f"B0si{tc}.{e}.{ic}")

                            if e < NE:
                                def f_h1(v, ic=ic, hp_e=hp_e, up_=up_, e=e):
                                    v.tensor_tensor(
                                        a32t[:, o_silu[ic % 2]:o_silu[ic % 2] + TC],
                                        a32t[:, o_silu[ic % 2]:o_silu[ic % 2] + TC],
                                        pb[up_][:, :], ALU.mult)
                                    v.drain()
                                    return v.tensor_tensor(
                                        a16t[:, hp_e + ic * TC: hp_e + (ic + 1) * TC],
                                        a32t[:, o_silu[ic % 2]:o_silu[ic % 2] + TC],
                                        pb[4 + e % 2][:, :], ALU.mult)
                                dh = [op_si, op_umm, prev.get(("cbc", e)), dn_last_prev]
                            else:
                                def f_h1(v, ic=ic, hp_e=hp_e, up_=up_):
                                    return v.tensor_tensor(
                                        a16t[:, hp_e + ic * TC: hp_e + (ic + 1) * TC],
                                        a32t[:, o_silu[ic % 2]:o_silu[ic % 2] + TC],
                                        pb[up_][:, :], ALU.mult)
                                dh = [op_si, op_umm, dn_last_prev]
                            op_h1 = P.add("V", f_h1, deps=dh, name=f"B0h{tc}.{e}.{ic}")
                            prev[("pg", gp_)] = op_h1
                            prev[("pu", up_)] = op_h1
                            prev[("h1p", ic % 2)] = op_h1
                            e_h1.append(op_h1)
                        prev[("gubuf", hidx)] = e_h1[-1]
                        hidx += 1

                    if e + 1 < NE:
                        def f_cbc(t, e=e, tc=tc):
                            return t.matmul(
                                pb[4 + (e + 1) % 2][:, :],
                                oneh_sb[0:16, (e + 1) * 128:(e + 2) * 128],
                                cfm_sb[0:16, tc * TC:(tc + 1) * TC],
                                start=True, stop=True)
                        dc = [A["c_last"], prev.get(("pcb", (e + 1) % 2))]
                        op_cbc = P.add("T", f_cbc, deps=dc, name=f"B0cbc{tc}.{e + 1}")
                        prev[("cbc", e + 1)] = op_cbc
                    if e < NE:
                        prev[("pcb", e % 2)] = e_h1[-1]
                    hlast.append(e_h1[-1])

                dn_ops = []
                for e in range(NXP):
                    wp = e % 2
                    if tc == 0:
                        dsrc = l0_d2[e] if e < NE else l0_sd2[e - NE]
                    else:
                        dsrc = s_d[e]

                    def f_ldd(q, dsrc=dsrc, wp=wp):
                        if q is None:
                            return 1
                        return q.dma_start(
                            out=a16t[:, o_db[wp]:o_db[wp] + 4096], in_=dsrc)
                    dld = [prev.get(("dnw", wp)), prev.get(("wbdb", wp))]
                    if tc == 0 and e < 2:
                        dld += ldfence
                    if tc == 0:
                        opldd = P.add("G", f_ldd, deps=dld,
                                      sem=sW[2 + wp], name=f"B0ldd{tc}.{e}")

                        def f_wbd(q, e=e, wp=wp):
                            if q is None:
                                return 1
                            return q.dma_start(
                                out=s_d[e],
                                in_=a16t[:, o_db[wp]:o_db[wp] + 4096])
                        wi = wb_i[0]
                        op_wbd = P.add("Q", f_wbd, deps=[opldd, wb_ring[wi % 4]],
                                       sem=sWB[wi % 4], name=f"B0wbd{e}")
                        wb_ring[wi % 4] = op_wbd
                        wb_i[0] += 1
                        wb_map[("d", e)] = op_wbd
                        prev[("wbdb", wp)] = op_wbd
                    else:
                        dld += [wb_map.get(("d", e))]
                        opldd = P.add("G" if wp == 0 else "Q", f_ldd, deps=dld,
                                      sem=(sW[2] if wp == 0 else sWd[1]),
                                      name=f"B0ldd{tc}.{e}")

                    def f_dn(t, e=e, wp=wp):
                        for hcc in range(NHC):
                            for ic in range(4):
                                li = t.matmul(
                                    pb[hcc][:, :],
                                    a16t[:, o_db[wp] + ic * 1024 + hcc * 128:
                                         o_db[wp] + ic * 1024 + hcc * 128 + 128],
                                    a16t[:, o_hp + e * 2048 + ic * TC:
                                         o_hp + e * 2048 + (ic + 1) * TC],
                                    start=(e == 0 and ic == 0),
                                    stop=(e == NXP - 1 and ic == 3))
                        return li
                    ddn = [opldd, hlast[e]]
                    if e == 0:
                        ddn += [hlast[-1] if False else hlast[0], gevac_prev]
                        ddn += hlast[:1]
                        ddn.append(hlast[NXP - 1])
                    op_dn = P.add("T", f_dn, deps=ddn, name=f"B0dn{tc}.{e}")
                    prev[("dnw", wp)] = op_dn
                    dn_ops.append(op_dn)
                dn_last_prev = dn_ops[-1]

                gev = []
                for hcc in range(NHC):
                    def f_gev(s, hcc=hcc, tc=tc):
                        return s.activation(
                            g_sb[:, hcc * T + tc * TC: hcc * T + tc * TC + TC],
                            pb[hcc][:, :], AF.Silu)
                    gev.append(P.add("S", f_gev, deps=[dn_ops[-1]],
                                     name=f"B0gev{tc}.{hcc}"))
                gevac_prev = gev[-1]

            a32.pop()
            a16.pop()
            return dict(g_last=gevac_prev, lasts=[gevac_prev, dn_last_prev])

        def phase_B1(A, ldfence):
            a16.push()
            o_gb = [a16.alloc(1024) for _ in range(2)]
            o_ub = [a16.alloc(1024) for _ in range(2)]
            o_db = [a16.alloc(3072) for _ in range(2)]
            o_hp = a16.alloc(NIC1 * 1024)
            a32.push()
            o_silu = [a32.alloc(TC) for _ in range(2)]

            prev = {}
            gevac_prev = None
            dmm_last = None
            for sc in range(2):
                h_all = []
                for ic in range(NIC1):
                    wp = ic % 2
                    gsrc = l1_g2[ic]
                    usrc = l1_u2[ic]

                    def f_ldg(q, gsrc=gsrc, wp=wp):
                        if q is None:
                            return 1
                        return q.dma_start(
                            out=a16t[:, o_gb[wp]:o_gb[wp] + 1024], in_=gsrc)

                    def f_ldu(q, usrc=usrc, wp=wp):
                        if q is None:
                            return 1
                        return q.dma_start(
                            out=a16t[:, o_ub[wp]:o_ub[wp] + 1024], in_=usrc)
                    gidx = sc * NIC1 + ic
                    bp = prev.get(("gubuf", gidx - 2))
                    fen = ldfence if gidx < 2 else []
                    opldg = P.add("G", f_ldg, deps=[bp] + fen, sem=sW[wp],
                                  name=f"B1ldg{sc}.{ic}")
                    opldu = P.add("G", f_ldu, deps=[bp] + fen, sem=sW[wp],
                                  name=f"B1ldu{sc}.{ic}")

                    h_ops = []
                    for tcc in range(2):
                        tcg = sc * 2 + tcc

                        def f_gmm(t, wp=wp, tcg=tcg, tcc=tcc):
                            for hc in range(NHC):
                                li = t.matmul(
                                    pb[tcc][:, :],
                                    a16t[:, o_gb[wp] + hc * 128: o_gb[wp] + (hc + 1) * 128],
                                    xn_sb[:, hc * T + tcg * TC: hc * T + tcg * TC + TC],
                                    start=(hc == 0), stop=(hc == NHC - 1))
                            return li
                        dg = [opldg, opldu, prev.get(("pg", tcc)), A["xn_last"]]
                        op_gmm = P.add("T", f_gmm, deps=dg, name=f"B1g{sc}.{ic}.{tcc}")

                        def f_umm(t, wp=wp, tcg=tcg, tcc=tcc):
                            for hc in range(NHC):
                                li = t.matmul(
                                    pb[2 + tcc][:, :],
                                    a16t[:, o_ub[wp] + hc * 128: o_ub[wp] + (hc + 1) * 128],
                                    xn_sb[:, hc * T + tcg * TC: hc * T + tcg * TC + TC],
                                    start=(hc == 0), stop=(hc == NHC - 1))
                            return li
                        du = [opldu, opldg, prev.get(("pu", tcc)), A["xn_last"]]
                        op_umm = P.add("T", f_umm, deps=du, name=f"B1u{sc}.{ic}.{tcc}")

                        def f_silu(s, tcc=tcc):
                            return s.activation(
                                a32t[:, o_silu[tcc]:o_silu[tcc] + TC],
                                pb[tcc][:, :], AF.Silu, scale=DQ)
                        op_si = P.add("S", f_silu, deps=[op_gmm, prev.get(("h1p", tcc))],
                                      name=f"B1si{sc}.{ic}.{tcc}")

                        def f_h(v, ic=ic, tcc=tcc):
                            return v.tensor_tensor(
                                a16t[:, o_hp + ic * 1024 + tcc * TC:
                                     o_hp + ic * 1024 + (tcc + 1) * TC],
                                a32t[:, o_silu[tcc]:o_silu[tcc] + TC],
                                pb[2 + tcc][:, :], ALU.mult)
                        dh = [op_si, op_umm]
                        if sc == 1 and ic < 2:
                            dh.append(dmm_last)
                        op_h = P.add("V", f_h, deps=dh, name=f"B1h{sc}.{ic}.{tcc}")
                        prev[("pg", tcc)] = op_h
                        prev[("pu", tcc)] = op_h
                        prev[("h1p", tcc)] = op_h
                        h_ops.append(op_h)
                    prev[("gubuf", gidx)] = h_ops[-1]
                    h_all += h_ops

                for hcc in range(NHC):
                    wp = hcc % 2
                    dsrc = l1_d2[hcc]

                    def f_ldd(q, dsrc=dsrc, wp=wp):
                        if q is None:
                            return 1
                        return q.dma_start(
                            out=a16t[:, o_db[wp]:o_db[wp] + 3072], in_=dsrc)
                    dld1 = [prev.get(("dnw", wp))]
                    if sc == 0 and hcc < 2:
                        dld1 += ldfence
                    opldd = P.add("G", f_ldd, deps=dld1,
                                  sem=sW[2 + wp], name=f"B1ldd{sc}.{hcc}")

                    for tcc in range(2):
                        tcg = sc * 2 + tcc

                        def f_dmm(t, wp=wp, tcc=tcc):
                            for ic in range(NIC1):
                                li = t.matmul(
                                    pb[4 + tcc][:, :],
                                    a16t[:, o_db[wp] + ic * 128: o_db[wp] + (ic + 1) * 128],
                                    a16t[:, o_hp + ic * 1024 + tcc * TC:
                                         o_hp + ic * 1024 + (tcc + 1) * TC],
                                    start=(ic == 0), stop=(ic == NIC1 - 1))
                            return li
                        ddm = [opldd, h_all[-1], prev.get(("gev", tcc))]
                        op_dmm = P.add("T", f_dmm, deps=ddm, name=f"B1d{sc}.{hcc}.{tcc}")
                        prev[("dnw", wp)] = op_dmm
                        dmm_last = op_dmm

                        def f_gev(s, hcc=hcc, tcg=tcg, tcc=tcc):
                            return s.activation(
                                g_sb[:, hcc * T + tcg * TC: hcc * T + tcg * TC + TC],
                                pb[4 + tcc][:, :], AF.Silu, scale=DQ * DQ)
                        op_gev = P.add("S", f_gev, deps=[op_dmm],
                                       name=f"B1gev{sc}.{hcc}.{tcc}")
                        prev[("gev", tcc)] = op_gev
                        gevac_prev = op_gev
            a32.pop()
            a16.pop()
            return dict(g_last=gevac_prev, lasts=[gevac_prev, dmm_last])

        def phase_C(L, A, B, l0_stores=None):
            nkv = 2 if L == 0 else 8
            wk_src = l0_wk2 if L == 0 else l1_wk2
            wv_src = l0_wv2 if L == 0 else l1_wv2
            wq_src = l0_wq2 if L == 0 else l1_wq2
            wo_src = l0_wo2 if L == 0 else l1_wo2
            vwid = 256 if L == 0 else 1024
            nvh = vwid // HD
            xsrc = x_in if L == 0 else x_res
            xdst = x_res if L == 0 else out_t

            a16.push()
            o_kfm = a16.alloc(nkv * T)
            o_vtm = a16.alloc(16 * 65 * nvh)
            o_qa = [a16.alloc(TC) for _ in range(2)]
            o_qb = [a16.alloc(TC) for _ in range(2)]
            o_qt = a16.alloc(TC) if L == 0 else None
            o_bvb = a16.alloc(1024) if L == 1 else None
            o_pt = [a16.alloc(2 * TC) for _ in range(3)]
            o_sh = a16.alloc(2 * NHC * TC)
            o_wv = o_sh
            o_oall = [o_sh, o_sh + NHC * TC]
            o_wq = [a16.alloc(1024) for _ in range(2)]
            o_wo = [a16.alloc(1024) for _ in range(2)]
            o_nrm = A16 - 1024
            o_cs = a16.alloc(2 * T) if L == 0 else None
            o_cm = a16.alloc(4 * TC) if L == 1 else None
            a32.push()
            o_sums = a32.alloc(2 * TC)
            o_rcf = a32.alloc(2 * TC)
            o_rt = a32.alloc(2 * TC)
            o_xin = [a32.alloc(TC) for _ in range(2)]
            o_xout = [a32.alloc(TC) for _ in range(2)]

            fence = B["lasts"] + [A["xn_last"]]
            pv = {}
            ops = {}

            if L == 0:
                def f_ldcs(q):
                    return 1 if q is None else q.dma_start(
                        out=a16t[:, o_cs:o_cs + 2 * T], in_=cs2[:, :])
                op_cs = P.add("G", f_ldcs, deps=fence, sem=sCS, name="C0ldcs")
            else:
                def f_ldcm(q):
                    return 1 if q is None else q.dma_start(
                        out=a16t[:, o_cm:o_cm + 4 * TC], in_=cmask[:, :])
                op_cs = P.add("G", f_ldcm, deps=fence, sem=sCS, name="C1ldcm")

            def f_msets(v):
                for tt in range(16):
                    v.memset(
                        a16t[:, o_vtm + tt * 65 * nvh:
                             o_vtm + (tt + 1) * 65 * nvh].rearrange(
                            "p (h w) -> p h w", w=65)[:, :, 64:65], 1.0)
                for i in range(2):
                    v.memset(a16t[64:128, o_qa[i]:o_qa[i] + TC], 0.0)
                    v.memset(a16t[0:64, o_qb[i]:o_qb[i] + TC], 0.0)
                return v.memset(a16t[:, o_nrm:o_nrm + 2 * TC], 0.0)
            op_ms = P.add("V", f_msets, deps=fence, name=f"C{L}msets")

            if L == 1:
                def f_ldbv(g):
                    return 1 if g is None else g.dma_start(
                        out=a16t[:, o_bvb:o_bvb + 1024], in_=l1_bvbc[:, :])
                op_bv = P.add("G", f_ldbv, deps=fence, sem=sCS, name="C1ldbv")

            kv_last = None
            for c in range(nkv):
                wp = c % 2
                ksrc = wk_src[c]

                def f_ldwk(q, ksrc=ksrc, wp=wp):
                    if q is None:
                        return 1
                    return q.dma_start(
                        out=a16t[:, o_wq[wp]:o_wq[wp] + 1024], in_=ksrc)
                opldk = P.add("G", f_ldwk, deps=[pv.get(("kbuf", c - 2))] + fence,
                              sem=sKW[c % 2], name=f"C{L}ldk{c}")

                for tc in range(NTC):
                    def f_kmm(t, wp=wp, tc=tc):
                        for hc in range(NHC):
                            li = t.matmul(
                                pb6[:, :],
                                a16t[:, o_wq[wp] + hc * 128: o_wq[wp] + (hc + 1) * 128],
                                xn_sb[:, hc * T + tc * TC: hc * T + tc * TC + TC],
                                start=(hc == 0), stop=(hc == NHC - 1))
                        return li
                    dk = [opldk, A["xn_last"], pv.get(("p6",))]
                    op_kmm = P.add("T", f_kmm, deps=dk, name=f"C{L}kmm{c}.{tc}")
                    pv[("kbuf", c)] = op_kmm

                    if L == 0:
                        def f_kev(v, c=c, tc=tc):
                            return v.tensor_copy(a16t[:, o_qt:o_qt + TC], pb6[:, :])
                        op_kev = P.add("V", f_kev,
                                       deps=[op_kmm, pv.get(("qtT",))],
                                       name=f"C0kev{c}.{tc}")
                        pv[("p6",)] = op_kev

                        def f_krot(t, c=c, tc=tc):
                            return t.matmul(pb7[:, :], cst16[:, B_R2T:B_R2T + 128],
                                            a16t[:, o_qt:o_qt + TC],
                                            start=True, stop=True)
                        op_krot = P.add("T", f_krot, deps=[op_kev, pv.get(("p7",))],
                                        name=f"C0krot{c}.{tc}")
                        pv[("qtT",)] = op_krot

                        def f_krope(v, c=c, tc=tc):
                            v.tensor_tensor(a32t[:, o_rt:o_rt + TC], pb7[:, :],
                                            a16t[:, o_cs + T + tc * TC:
                                                 o_cs + T + (tc + 1) * TC], ALU.mult)
                            v.drain()
                            v.tensor_tensor(a32t[:, o_rt + TC:o_rt + 2 * TC],
                                            a16t[:, o_qt:o_qt + TC],
                                            a16t[:, o_cs + tc * TC: o_cs + (tc + 1) * TC],
                                            ALU.mult)
                            v.drain()
                            return v.tensor_tensor(
                                a16t[:, o_kfm + c * T + tc * TC:
                                     o_kfm + c * T + (tc + 1) * TC],
                                a32t[:, o_rt:o_rt + TC],
                                a32t[:, o_rt + TC:o_rt + 2 * TC], ALU.add)
                        op_kr = P.add("V", f_krope, deps=[op_krot, op_cs],
                                      name=f"C0krope{c}.{tc}")
                        pv[("p7",)] = op_kr
                        kv_last = op_kr
                    else:
                        def f_kev(v, c=c, tc=tc):
                            return v.tensor_scalar(
                                a16t[:, o_kfm + c * T + tc * TC:
                                     o_kfm + c * T + (tc + 1) * TC],
                                pb6[:, :],
                                cst32[:, C_BQK + 8 + c: C_BQK + 9 + c],
                                None, ALU.add)
                        op_kev = P.add("V", f_kev, deps=[op_kmm], name=f"C1kev{c}.{tc}")
                        pv[("p6",)] = op_kev
                        kv_last = op_kev
            op_kevl = pv[("p6",)]

            def f_ldwv(q):
                if q is None:
                    return 1
                return q.dma_start(
                    out=a16t[:, o_wv:o_wv + NHC * vwid], in_=wv_src)
            opldv = P.add("G", f_ldwv, deps=fence, sem=sVW, name=f"C{L}ldv")

            nhalf = max(1, vwid // 512)
            wvw = min(512, vwid)
            v_last = None
            op_vmm_last = None
            for tt in range(16):
                for hf in range(nhalf):
                    w0 = hf * 512

                    def f_vmm(t, tt=tt, w0=w0):
                        for hc in range(NHC):
                            li = t.matmul(
                                pb7[0:128, 0:wvw],
                                xn_sb[:, hc * T + tt * 128: hc * T + (tt + 1) * 128],
                                a16t[:, o_wv + hc * vwid + w0:
                                     o_wv + hc * vwid + w0 + wvw],
                                start=(hc == 0), stop=(hc == NHC - 1))
                        return li
                    dvm = [opldv, A["xn_last"], pv.get(("p7",))]
                    op_vmm = P.add("T", f_vmm, deps=dvm, name=f"C{L}vmm{tt}.{hf}")
                    op_vmm_last = op_vmm

                    def f_vev(v, tt=tt, w0=w0):
                        nh_here = wvw // HD
                        dst = a16t[:, o_vtm + tt * 65 * nvh + (w0 // HD) * 65:
                                   o_vtm + tt * 65 * nvh + (w0 // HD) * 65 + nh_here * 65]
                        dst3 = dst.rearrange("p (h w) -> p h w", w=65)[:, :, 0:64]
                        src3 = pb7[0:128, 0:wvw].rearrange("p (h w) -> p h w", w=64)
                        if L == 1:
                            return v.tensor_tensor(
                                dst3, src3,
                                a16t[:, o_bvb + w0:o_bvb + w0 + wvw].rearrange(
                                    "p (h w) -> p h w", w=64), ALU.add)
                        return v.tensor_copy(dst3, src3)
                    dve = [op_vmm, op_ms]
                    if L == 1:
                        dve.append(op_bv)
                    op_vev = P.add("V", f_vev, deps=dve,
                                   name=f"C{L}vev{tt}.{hf}")
                    pv[("p7",)] = op_vev
                    v_last = op_vev
            op_vevl = v_last

            pend = []
            curg = [None, []]

            def pbo_of(fc):
                return pb6 if fc % 2 == 0 else pb5

            def make_oproj_units(qcd, norm_deps):
                groups = []
                for fc in range(NHC):
                    wp = fc % 2
                    xp = fc % 2
                    units = []

                    def u_ld(fc=fc, wp=wp, qcd=qcd):
                        def f_ldwo(q, fc=fc, wp=wp):
                            if q is None:
                                return 1
                            return q.dma_start(
                                out=a16t[:, o_wo[wp]:o_wo[wp] + 1024],
                                in_=wo_src[fc])
                        prv = ops.get(("opj", fc - 2, 7))
                        if prv is None:
                            prv = ops.get(("opj", fc + 6, 7))
                        dl = [prv] if prv is not None else list(fence)
                        ops[("ldwo", fc)] = P.add("G", f_ldwo, deps=dl,
                                                  sem=sWO[fc % 2],
                                                  name=f"C{L}ldwo{qcd}.{fc}")
                    units.append(u_ld)
                    for p2 in range(8):
                        def u_mm(fc=fc, p2=p2, wp=wp, qcd=qcd):
                            def f_mm(t, fc=fc, p2=p2, wp=wp, qcd=qcd):
                                return t.matmul(
                                    pbo_of(fc)[:, :],
                                    a16t[:, o_wo[wp] + p2 * 128:
                                         o_wo[wp] + p2 * 128 + 128],
                                    a16t[:, o_oall[qcd % 2] + p2 * TC:
                                         o_oall[qcd % 2] + (p2 + 1) * TC],
                                    start=(p2 == 0), stop=(p2 == 7))
                            dm = [ops[("ldwo", fc)]]
                            if p2 == 0:
                                prvb = ops.get(("fuse", fc - 2))
                                if prvb is None:
                                    prvb = ops.get(("fuse", fc + 6))
                                dm += [prvb] + norm_deps
                                dm += pv.get(("normd", 1 - fc % 2), [])
                                dm.append(pv.get(("rcps", 1 - fc % 2)))
                                if fc == 0:
                                    dm.append(op_kevl)
                                if fc == 1:
                                    dm.append(op_vevl)
                            ops[("opj", fc, p2)] = P.add(
                                "T", f_mm, deps=dm, name=f"C{L}op{qcd}.{fc}.{p2}")
                            ops[("opjlast",)] = ops[("opj", fc, p2)]
                        units.append(u_mm)

                    def u_fin(fc=fc, xp=xp, qcd=qcd):
                        def f_ldx(q, qcd=qcd, fc=fc, xp=xp):
                            if q is None:
                                return 1
                            return q.dma_start(
                                out=a32t[:, o_xin[xp]:o_xin[xp] + TC],
                                in_=xsrc[fc * 128:(fc + 1) * 128,
                                         qcd * TC:(qcd + 1) * TC])
                        prvf = ops.get(("fuse", fc - 2))
                        if prvf is None:
                            prvf = ops.get(("fuse", fc + 6))
                        dlx = [prvf]
                        if L == 1 and qcd == 0 and fc < 2:
                            dlx += l0_stores
                        if qcd == 0 and fc < 2:
                            dlx += fence
                        op_ldx = P.add("Q", f_ldx, deps=dlx, sem=sXl[xp],
                                       name=f"C{L}ldx{qcd}.{fc}")

                        def f_fuse(v, fc=fc, qcd=qcd, xp=xp, L=L):
                            bias_ap = (cst32[:, C_BOUT + fc: C_BOUT + fc + 1]
                                       if L == 1 else 0.0)
                            v.scalar_tensor_tensor(
                                a32t[:, o_xout[xp]:o_xout[xp] + TC],
                                pbo_of(fc)[:, :], bias_ap,
                                g_sb[:, fc * T + qcd * TC: fc * T + qcd * TC + TC],
                                ALU.add, ALU.mult)
                            v.drain()
                            return v.tensor_tensor(
                                a32t[:, o_xout[xp]:o_xout[xp] + TC],
                                a32t[:, o_xout[xp]:o_xout[xp] + TC],
                                a32t[:, o_xin[xp]:o_xin[xp] + TC], ALU.add)
                        dfu = [ops[("opj", fc, 7)], op_ldx, B["g_last"],
                               ops.get(("stx", xp))]
                        op_fuse = P.add("V", f_fuse, deps=dfu,
                                        name=f"C{L}fu{qcd}.{fc}")
                        ops[("fuse", fc)] = op_fuse
                        ops[("fusel",)] = op_fuse
                        pv[("lastfuse",)] = op_fuse

                        def f_stx(q, qcd=qcd, fc=fc, xp=xp):
                            if q is None:
                                return 1
                            return q.dma_start(
                                out=xdst[fc * 128:(fc + 1) * 128,
                                         qcd * TC:(qcd + 1) * TC],
                                in_=a32t[:, o_xout[xp]:o_xout[xp] + TC])
                        ops[("stx", xp)] = P.add("Q", f_stx, deps=[op_fuse],
                                                 sem=(sO if xp == 0 else sO2),
                                                 name=f"C{L}stx{qcd}.{fc}")
                    units.append(u_fin)
                    groups.append(units)
                return groups

            def start_group():
                if curg[0] is None and pend:
                    g = pend.pop(0)
                    g[0]()
                    curg[0] = g
                    curg[1] = list(g[1:-1])

            def emit_unit():
                if curg[0] is not None and curg[1]:
                    curg[1].pop(0)()

            def finish_group():
                if curg[0] is not None:
                    while curg[1]:
                        curg[1].pop(0)()
                    curg[0][-1]()
                    curg[0] = None

            epi_pend = [None]
            epi1_pend = [None]

            def flush_epi():
                if epi_pend[0] is not None:
                    epi_pend[0]()
                    epi_pend[0] = None

            def flush_epi1():
                if epi1_pend[0] is not None:
                    epi1_pend[0]()
                    epi1_pend[0] = None

            for qc in range(NTC):
                pv[("oallfree",)] = ops.get(("opjlast",))
                nkc = 16 if L == 0 else 4 * qc + 4
                for p in range(8):
                    wp = p % 2
                    qi = qc * 8 + p
                    pi = qi % 2
                    av_a, av_b = (pb4, pb5) if pi == 0 else (pb6, pb7)
                    qsrc = wq_src[p]

                    def f_ldwq(q, qsrc=qsrc, wp=wp):
                        if q is None:
                            return 1
                        return q.dma_start(
                            out=a16t[:, o_wq[wp]:o_wq[wp] + 1024], in_=qsrc)
                    dlq = [pv.get(("qbuf", qi - 2))]
                    if qc == 0 and p < 2:
                        dlq.append(pv.get(("kbuf", nkv - 2 + p)))
                    opldq = P.add("G", f_ldwq, deps=dlq, sem=sQW[p % 2],
                                  name=f"C{L}ldq{qc}.{p}")

                    def f_qmm(t, wp=wp, qc=qc):
                        for hc in range(NHC):
                            li = t.matmul(
                                pst1[:, 0:512],
                                a16t[:, o_wq[wp] + hc * 128: o_wq[wp] + (hc + 1) * 128],
                                xn_sb[:, hc * T + qc * TC: hc * T + qc * TC + TC],
                                start=(hc == 0), stop=(hc == NHC - 1))
                        return li
                    dq = [opldq, A["xn_last"], pv.get(("exp", 1))]
                    if qi == 0:
                        dq += fence + [pv.get(("kbuf", nkv - 1))]
                    op_qmm = P.add("T", f_qmm, deps=dq, name=f"C{L}qmm{qc}.{p}")
                    pv[("qbuf", qi)] = op_qmm

                    if L == 0:
                        def f_qev(v):
                            return v.tensor_copy(a16t[:, o_qt:o_qt + TC],
                                                 pst1[:, 0:512])
                        op_qev = P.add("V", f_qev, deps=[op_qmm, pv.get(("qtT",))],
                                       name=f"C0qev{qc}.{p}")

                        def f_qrot(t):
                            return t.matmul(pst1[:, 512:1024],
                                            cst16[:, B_R2T:B_R2T + 128],
                                            a16t[:, o_qt:o_qt + TC],
                                            start=True, stop=True)
                        op_qrot = P.add("T", f_qrot,
                                        deps=[op_qev, pv.get(("exp", 1))],
                                        name=f"C0qrot{qc}.{p}")
                        pv[("qtT",)] = op_qrot

                        def f_qrope(v, qc=qc, wp=wp):
                            v.tensor_tensor(a32t[:, o_rt:o_rt + TC],
                                            pst1[:, 512:1024],
                                            a16t[:, o_cs + T + qc * TC:
                                                 o_cs + T + (qc + 1) * TC], ALU.mult)
                            v.drain()
                            v.tensor_tensor(a32t[:, o_rt + TC:o_rt + 2 * TC],
                                            a16t[:, o_qt:o_qt + TC],
                                            a16t[:, o_cs + qc * TC:
                                                 o_cs + (qc + 1) * TC], ALU.mult)
                            v.drain()
                            v.tensor_tensor(a16t[0:64, o_qa[wp]:o_qa[wp] + TC],
                                            a32t[0:64, o_rt:o_rt + TC],
                                            a32t[0:64, o_rt + TC:o_rt + 2 * TC],
                                            ALU.add)
                            v.drain()
                            return v.tensor_tensor(
                                a16t[64:128, o_qb[wp]:o_qb[wp] + TC],
                                a32t[64:128, o_rt:o_rt + TC],
                                a32t[64:128, o_rt + TC:o_rt + 2 * TC], ALU.add)
                        dqr = [op_qrot, op_cs, op_ms, pv.get(("stl", wp))]
                        q_ready = P.add("V", f_qrope, deps=dqr,
                                        name=f"C0qrope{qc}.{p}")
                    else:
                        def f_qev(v, p=p, wp=wp):
                            v.tensor_scalar(a16t[0:64, o_qa[wp]:o_qa[wp] + TC],
                                            pst1[0:64, 0:512],
                                            cst32[0:64, C_BQK + p: C_BQK + p + 1],
                                            None, ALU.add)
                            v.drain()
                            return v.tensor_scalar(
                                a16t[64:128, o_qb[wp]:o_qb[wp] + TC],
                                pst1[64:128, 0:512],
                                cst32[64:128, C_BQK + p: C_BQK + p + 1],
                                None, ALU.add)
                        dqr = [op_qmm, op_ms, pv.get(("stl", wp))]
                        q_ready = P.add("V", f_qev, deps=dqr,
                                        name=f"C1qev{qc}.{p}")

                    if L == 0:
                        kch = p // 4
                        kva, kvb = 2 * (p // 4), 2 * (p // 4) + 1
                    else:
                        kch = p
                        kva, kvb = 2 * p, 2 * p + 1

                    flush_epi1()
                    start_group()
                    exp_ops = {}

                    def emit_av(kc, kva=kva, kvb=kvb, nkc=nkc, qc=qc, p=p,
                                av_a=av_a, av_b=av_b, pi=pi, qi=qi):
                        kp4 = kc % 2
                        pt_d = exp_ops[kc][1]

                        def f_av(t, kc=kc, kp4=kp4, kva=kva, kvb=kvb, nkc=nkc,
                                 av_a=av_a, av_b=av_b):
                            t.matmul(av_a[0:65, :],
                                     a16t[:, o_vtm + kc * 65 * nvh + kva * 65:
                                          o_vtm + kc * 65 * nvh + kva * 65 + 65],
                                     a16t[:, o_pt[kp4]:o_pt[kp4] + TC],
                                     start=(kc == 0), stop=(kc == nkc - 1))
                            return t.matmul(
                                av_b[0:65, :],
                                a16t[:, o_vtm + kc * 65 * nvh + kvb * 65:
                                     o_vtm + kc * 65 * nvh + kvb * 65 + 65],
                                a16t[:, o_pt[kp4] + TC:o_pt[kp4] + 2 * TC],
                                start=(kc == 0), stop=(kc == nkc - 1))
                        dav = [pt_d]
                        if kc == 0:
                            dav += [v_last, op_ms, pv.get(("lastfuse",))]
                            dav += pv.get(("normd", pi), [])
                            if qi == 1:
                                dav += [op_kevl, op_vevl]
                        op = P.add("T", f_av, deps=dav, name=f"C{L}av{qc}.{p}.{kc}")
                        pv[("avpt", kp4)] = op
                        return op

                    op_st = None
                    for kc in range(nkc):
                        kp = kc % 2

                        def f_st(t, kc=kc, kp=kp, kch=kch, wp=wp):
                            t.matmul(pst[kp][:, 0:512],
                                     a16t[:, o_kfm + kch * T + kc * 128:
                                          o_kfm + kch * T + kc * 128 + 128],
                                     a16t[:, o_qa[wp]:o_qa[wp] + TC],
                                     start=True, stop=True)
                            return t.matmul(
                                pst[kp][:, 512:1024],
                                a16t[:, o_kfm + kch * T + kc * 128:
                                     o_kfm + kch * T + kc * 128 + 128],
                                a16t[:, o_qb[wp]:o_qb[wp] + TC],
                                start=True, stop=True)
                        dst_ = [q_ready, kv_last]
                        if kc >= 2:
                            dst_.append(exp_ops[kc - 2][0])
                        if kc == 0:
                            dst_.append(pv.get(("exp", 0)))
                        op_st = P.add("T", f_st, deps=dst_,
                                      name=f"C{L}st{qc}.{p}.{kc}")
                        if kc == nkc - 1:
                            pv[("stl", wp)] = op_st

                        if kc == 1:
                            flush_epi()
                            start_group()
                        if kc >= 1:
                            emit_av(kc - 1)
                        if kc >= 4:
                            emit_unit()
                            if kc >= nkc - 2:
                                emit_unit()

                        kp4 = kc % 2

                        def f_exp(s, kc=kc, kp=kp, kp4=kp4):
                            return s.activation(
                                a16t[:, o_pt[kp4]:o_pt[kp4] + 2 * TC],
                                pst[kp][:, 0:1024], AF.Exp, scale=0.125)
                        de = [op_st, pv.get(("avpt", kp4))]
                        op_exp = P.add("S", f_exp, deps=de,
                                       name=f"C{L}exp{qc}.{p}.{kc}")
                        pt_dep = op_exp

                        if L == 1 and kc >= 4 * qc:
                            j = kc - 4 * qc

                            def f_msk(v, kp4=kp4, j=j):
                                v.tensor_tensor(a16t[:, o_pt[kp4]:o_pt[kp4] + TC],
                                                a16t[:, o_pt[kp4]:o_pt[kp4] + TC],
                                                a16t[:, o_cm + j * TC:
                                                     o_cm + (j + 1) * TC], ALU.mult)
                                v.drain()
                                return v.tensor_tensor(
                                    a16t[:, o_pt[kp4] + TC:o_pt[kp4] + 2 * TC],
                                    a16t[:, o_pt[kp4] + TC:o_pt[kp4] + 2 * TC],
                                    a16t[:, o_cm + j * TC: o_cm + (j + 1) * TC],
                                    ALU.mult)
                            pt_dep = P.add("V", f_msk, deps=[op_exp, op_cs],
                                           name=f"C1msk{qc}.{p}.{kc}")
                        exp_ops[kc] = (op_exp, pt_dep)

                    op_avl = emit_av(nkc - 1)
                    finish_group()
                    pv[("exp", 0)] = exp_ops[nkc - 2][0]
                    pv[("exp", 1)] = exp_ops[nkc - 1][0]

                    blk = o_oall[qc % 2] + p * TC

                    def mk_epi1(op_avl=op_avl, blk=blk, qc=qc, p=p,
                                av_a=av_a, av_b=av_b, pi=pi):
                      def emit1():
                        def f_oeva(v, blk=blk):
                            return v.tensor_copy(a16t[0:64, blk:blk + TC],
                                                 av_a[0:64, :])
                        dea = [op_avl, pv.get(("oallfree",))]
                        if qc <= 1 and p == 0:
                            dea.append(op_vmm_last)
                        op_oeva = P.add("V", f_oeva, deps=dea, name=f"C{L}oea{qc}.{p}")

                        def f_oevb(v, blk=blk):
                            return v.tensor_copy(a16t[64:128, blk:blk + TC],
                                                 av_b[0:64, :])
                        op_oevb = P.add("V", f_oevb, deps=[op_avl],
                                        name=f"C{L}oeb{qc}.{p}")

                        def f_rcp(s):
                            s.activation(a32t[0:1, o_rcf:o_rcf + TC],
                                         av_a[64:65, :], AF.Ln)
                            s.drain()
                            s.activation(a32t[0:1, o_rcf + TC:o_rcf + 2 * TC],
                                         av_b[64:65, :], AF.Ln)
                            s.drain()
                            return s.activation(a16t[0:1, o_nrm:o_nrm + 2 * TC],
                                                a32t[0:1, o_rcf:o_rcf + 2 * TC],
                                                AF.Exp, scale=-1.0)
                        op_rcp = P.add("S", f_rcp, deps=[op_avl, pv.get(("bcd",))],
                                       name=f"C{L}rcp{qc}.{p}")
                        op_sums = op_rcp
                        pv[("rcps", pi)] = op_rcp

                        def mk_epi(op_rcp=op_rcp, blk=blk, qc=qc, p=p,
                                   av_a=av_a, av_b=av_b, pi=pi,
                                   pbread=(op_oeva, op_oevb, op_sums)):
                            def emit():
                                def f_bc(t):
                                    t.matmul(av_a[:, :], cst16[:, B_BR0:B_BR0 + 128],
                                             a16t[:, o_nrm:o_nrm + TC],
                                             start=True, stop=True)
                                    return t.matmul(
                                        av_b[:, :], cst16[:, B_BR0:B_BR0 + 128],
                                        a16t[:, o_nrm + TC:o_nrm + 2 * TC],
                                        start=True, stop=True)
                                op_bc = P.add("T", f_bc,
                                              deps=[op_rcp] + list(pbread),
                                              name=f"C{L}bc{qc}.{p}")
                                pv[("bcd",)] = op_bc

                                def f_na(v, blk=blk):
                                    return v.tensor_tensor(
                                        a16t[0:64, blk:blk + TC],
                                        a16t[0:64, blk:blk + TC],
                                        av_a[0:64, :], ALU.mult)
                                op_na = P.add("V", f_na, deps=[op_bc],
                                              name=f"C{L}na{qc}.{p}")

                                def f_nb(v, blk=blk):
                                    return v.tensor_tensor(
                                        a16t[64:128, blk:blk + TC],
                                        a16t[64:128, blk:blk + TC],
                                        av_b[64:128, :], ALU.mult)
                                op_nb = P.add("V", f_nb, deps=[op_bc],
                                              name=f"C{L}nb{qc}.{p}")
                                pv[("normd", pi)] = [op_na, op_nb]
                                if p == 7:
                                    pend.extend(make_oproj_units(qc, [op_na, op_nb]))
                            return emit
                        epi_pend[0] = mk_epi()
                      return emit1
                    epi1_pend[0] = mk_epi1()

                while pend or curg[0] is not None:
                    start_group()
                    finish_group()

            flush_epi1()
            flush_epi()
            while pend or curg[0] is not None:
                start_group()
                finish_group()

            stores = [ops[("stx", 0)], ops[("stx", 1)]]
            res = dict(stores=stores,
                       lasts=[ops[("opjlast",)], ops[("fusel",)]] + stores)
            a32.pop()
            a16.pop()
            return res


        a16.push()
        o_cfm = a16.alloc(T)
        cfm_sb = a16t[:, o_cfm:o_cfm + T]

        A0 = phase_A(0, fence0)
        if debug and d_xn[0] is not None:
            def f_dxn0(q):
                if q is None:
                    return 1
                return q.dma_start(out=d_xn[0].rearrange("(c p) t -> p c t", p=128),
                                   in_=xn_sb[:, :].rearrange("p (c t) -> p c t", t=T))
            dbg_prev[0] = P.add("G", f_dxn0, deps=[A0["xn_last"], dbg_prev[0]],
                                sem=sDbg, name="d_xn0")
            A0["lasts"].append(dbg_prev[0])

            def f_dC(q):
                if q is None:
                    return 1
                return q.dma_start(out=d_C[:, :], in_=cfm_sb[0:16, :])
            dbg_prev[0] = P.add("G", f_dC, deps=[A0["c_last"], dbg_prev[0]],
                                sem=sDbg, name="d_C0")
            A0["lasts"].append(dbg_prev[0])

        B0 = phase_B0(A0, fence0)
        if debug and d_g[0] is not None:
            def f_dg0(q):
                if q is None:
                    return 1
                return q.dma_start(out=d_g[0].rearrange("(c p) t -> p c t", p=128),
                                   in_=g_sb[:, :].rearrange("p (c t) -> p c t", t=T))
            dbg_prev[0] = P.add("G", f_dg0, deps=[B0["g_last"], dbg_prev[0]],
                                sem=sDbg, name="d_g0")
            B0["lasts"].append(dbg_prev[0])
        a16.pop()

        C0 = phase_C(0, A0, B0)

        A1 = phase_A(1, C0["lasts"])
        if debug and d_xn[1] is not None:
            def f_dxn1(q):
                if q is None:
                    return 1
                return q.dma_start(out=d_xn[1].rearrange("(c p) t -> p c t", p=128),
                                   in_=xn_sb[:, :].rearrange("p (c t) -> p c t", t=T))
            dbg_prev[0] = P.add("G", f_dxn1, deps=[A1["xn_last"], dbg_prev[0]],
                                sem=sDbg, name="d_xn1")
            A1["lasts"].append(dbg_prev[0])

        B1 = phase_B1(A1, C0["lasts"])
        if debug and d_g[1] is not None:
            def f_dg1(q):
                if q is None:
                    return 1
                return q.dma_start(out=d_g[1].rearrange("(c p) t -> p c t", p=128),
                                   in_=g_sb[:, :].rearrange("p (c t) -> p c t", t=T))
            dbg_prev[0] = P.add("G", f_dg1, deps=[B1["g_last"], dbg_prev[0]],
                                sem=sDbg, name="d_g1")
            B1["lasts"].append(dbg_prev[0])

        C1 = phase_C(1, A1, B1, l0_stores=C0["stores"])

        q_sems = sXa + sWu + sWB + [sWd[1], sXb[0], sXb[1], sXl[0], sXl[1], sO, sO2, sDbq]
        g_sems = [sCst, sDbg, sCS, sVW] + sWO + sW + sKW + sQW

        def f_qfin(q):
            if q is None:
                return 0
            for sm in q_sems:
                if sm.v > 0:
                    q.wait_ge(sm.h, sm.v)
            return None

        def f_gfin(g):
            if g is None:
                return 0
            for sm in g_sems:
                if sm.v > 0:
                    g.wait_ge(sm.h, sm.v)
            return None
        P.add("Q", f_qfin, deps=C1["stores"], sem=sO, name="final_wait_q")
        P.add("G", f_gfin, deps=[], sem=sCst, name="final_wait_g")

        P.emit(block)

    return nc, dbg


def _host_inputs(inputs):
    f32 = np.float32
    bf = ml_dtypes.bfloat16
    x = np.asarray(inputs["L0_attn_x"], f32)

    qperm = np.empty(D, np.int64)
    for j in range(D):
        p, s, d = j // 128, (j % 128) // 64, j % 64
        qperm[j] = L0_PAIRS[p][s] * 64 + d
    wq_r = np.ascontiguousarray(np.asarray(inputs["L0_attn_Wq"], f32)[:, qperm])
    wo_r = np.ascontiguousarray(np.asarray(inputs["L0_attn_Wo"], f32)[qperm, :])

    gT = np.transpose(np.asarray(inputs["L0_ffn_gate_w"], f32), (0, 2, 1))
    uT = np.transpose(np.asarray(inputs["L0_ffn_up_w"], f32), (0, 2, 1))
    dT = np.transpose(np.asarray(inputs["L0_ffn_down_w"], f32), (0, 2, 1))
    sgT = np.asarray(inputs["L0_ffn_sh_gate_w"], f32).T
    suT = np.asarray(inputs["L0_ffn_sh_up_w"], f32).T
    sdT = np.asarray(inputs["L0_ffn_sh_down_w"], f32).T
    g2 = np.ascontiguousarray(gT.reshape(16, 8, 128, 2, 256)
                              .transpose(0, 3, 2, 1, 4).reshape(16, 2, 128, 2048))
    u2 = np.ascontiguousarray(uT.reshape(16, 8, 128, 2, 256)
                              .transpose(0, 3, 2, 1, 4).reshape(16, 2, 128, 2048))
    d2 = np.ascontiguousarray(dT.reshape(16, 4, 128, 1024)
                              .transpose(0, 2, 1, 3).reshape(16, 128, 4096))
    sg2 = np.ascontiguousarray(sgT.reshape(8, 128, 2, 2, 256)
                               .transpose(2, 3, 1, 0, 4).reshape(2, 2, 128, 2048))
    su2 = np.ascontiguousarray(suT.reshape(8, 128, 2, 2, 256)
                               .transpose(2, 3, 1, 0, 4).reshape(2, 2, 128, 2048))
    sd2 = np.ascontiguousarray(sdT.reshape(2, 4, 128, 1024)
                               .transpose(0, 2, 1, 3).reshape(2, 128, 4096))
    wq2 = np.ascontiguousarray(wq_r.reshape(8, 128, 8, 128)
                               .transpose(2, 1, 0, 3).reshape(8, 128, 1024))
    wo2 = np.ascontiguousarray(wo_r.reshape(8, 128, 8, 128)
                               .transpose(2, 1, 0, 3).reshape(8, 128, 1024))
    wkl0 = np.asarray(inputs["L0_attn_Wk"], f32)
    wvl0 = np.asarray(inputs["L0_attn_Wv"], f32)
    wk2 = np.ascontiguousarray(wkl0.reshape(8, 128, 2, 128)
                               .transpose(2, 1, 0, 3).reshape(2, 128, 1024))
    wv2 = np.ascontiguousarray(wvl0.reshape(8, 128, 256)
                               .transpose(1, 0, 2).reshape(128, 2048))
    rwT = np.ascontiguousarray(np.asarray(inputs["L0_ffn_router_weight"], f32).T)
    rwp = np.zeros((128, 1024), bf)
    for hc in range(8):
        rwp[:, hc * 128: hc * 128 + 16] = rwT[hc * 128:(hc + 1) * 128, :].astype(bf)
    ebc = np.ascontiguousarray(np.broadcast_to(
        np.asarray(inputs["L0_ffn_e_bias"], f32)[None, :], (128, 16)))

    wqkv = np.asarray(inputs["L1_attn_W_qkv"], f32)
    bqkv = np.asarray(inputs["L1_attn_b_qkv"], f32)
    bqk = np.ascontiguousarray(bqkv[:2048].reshape(16, 128).T)
    bvbc = np.ascontiguousarray(np.broadcast_to(bqkv[2048:][None, :], (128, D)))
    wout = np.asarray(inputs["L1_attn_W_out"], f32)
    bout = np.ascontiguousarray(np.asarray(inputs["L1_attn_b_out"], f32).reshape(8, 128).T)
    wq12 = np.ascontiguousarray(wqkv[:, 0:1024].reshape(8, 128, 8, 128)
                                .transpose(2, 1, 0, 3).reshape(8, 128, 1024))
    wk12 = np.ascontiguousarray(wqkv[:, 1024:2048].reshape(8, 128, 8, 128)
                                .transpose(2, 1, 0, 3).reshape(8, 128, 1024))
    wv12 = np.ascontiguousarray(wqkv[:, 2048:3072].reshape(8, 128, 1024)
                                .transpose(1, 0, 2).reshape(128, 8192))
    wo12 = np.ascontiguousarray(wout.reshape(8, 128, 8, 128)
                                .transpose(2, 1, 0, 3).reshape(8, 128, 1024))
    g1T = np.asarray(inputs["L1_ffn_w_gate"], np.int32).T
    u1T = np.asarray(inputs["L1_ffn_w_up"], np.int32).T
    d1T = np.asarray(inputs["L1_ffn_w_down"], np.int32).T
    g12 = np.ascontiguousarray(g1T.reshape(8, 128, 24, 128)
                               .transpose(2, 1, 0, 3).reshape(24, 128, 1024))
    u12 = np.ascontiguousarray(u1T.reshape(8, 128, 24, 128)
                               .transpose(2, 1, 0, 3).reshape(24, 128, 1024))
    d12 = np.ascontiguousarray(d1T.reshape(24, 128, 8, 128)
                               .transpose(2, 1, 0, 3).reshape(8, 128, 3072))

    R2 = np.zeros((128, 128), f32)
    for s in range(2):
        for d in range(32):
            R2[s * 64 + d, s * 64 + d + 32] = -1.0
            R2[s * 64 + d + 32, s * 64 + d] = 1.0
    r2t = np.ascontiguousarray(R2.T).astype(bf)

    inv = 1.0 / (10000.0 ** (np.arange(0, HD, 2, dtype=np.float64) / HD))
    tpos = np.arange(T, dtype=np.float64)
    fr = np.outer(tpos, inv)
    emb = np.concatenate([fr, fr], axis=1)
    cos_t = np.cos(emb).astype(f32)
    sin_t = np.sin(emb).astype(f32)
    cs = np.zeros((128, 2 * T), f32)
    for s in range(2):
        cs[s * 64:(s + 1) * 64, :T] = cos_t.T
        cs[s * 64:(s + 1) * 64, T:] = sin_t.T

    cm = np.zeros((128, 4 * TC), f32)
    for j in range(4):
        kk = np.arange(128)[:, None] + 128 * j
        qq = np.arange(TC)[None, :]
        cm[:, j * TC:(j + 1) * TC] = (kk <= qq).astype(f32)

    shared = dict(
        l0_wq2=wq2, l0_wk2=wk2, l0_wv2=wv2, l0_wo2=wo2, l0_eb=ebc,
        l0_g2=g2, l0_u2=u2, l0_d2=d2, l0_sg2=sg2, l0_su2=su2, l0_sd2=sd2,
        l1_wq2=wq12, l1_wk2=wk12, l1_wv2=wv12, l1_wo2=wo12,
        l1_bqk=bqk, l1_bvbc=bvbc, l1_bout=bout,
        l1_g2=g12, l1_u2=u12, l1_d2=d12,
        ident=np.eye(128, dtype=f32),
        colones=np.ascontiguousarray(
            (np.arange(128)[None, :] == 0).astype(f32) *
            np.ones((128, 1), f32)),
        rwp_b=rwp,
        r2t=r2t, cs2=cs, cmask=cm.astype(bf),
        epsc=np.full((128, 1), EPS, f32),
        br0=np.ascontiguousarray(
            (np.arange(128)[:, None] == 0).astype(bf) *
            np.ones((1, 128), bf)),
        oneh=np.ascontiguousarray(
            (np.arange(16)[:, None] == (np.arange(16 * 128) // 128)[None, :])
            .astype(bf)),
    )
    in_maps = []
    for c in range(NCORE):
        m = dict(shared)
        m["x_in"] = np.ascontiguousarray(x[c].T)
        in_maps.append(m)
    return in_maps


def kernel(**inputs):
    in_maps = _host_inputs(inputs)
    nc, _ = build_program(debug=False)
    res = run_bass_kernel_spmd(nc, in_maps, core_ids=list(range(NCORE)))
    outs = [np.ascontiguousarray(np.asarray(r["out"]).T) for r in res.results]
    return np.stack(outs, axis=0).astype(np.float32)
